# version 04
# baseline (speedup 1.0000x reference)
import numpy as np
import ml_dtypes
import concourse.bass as bass
import concourse.mybir as mybir
from concourse.bass import IndirectOffsetOnAxis
from concourse.bass_utils import run_bass_kernel_spmd

F32 = mybir.dt.float32
BF16 = mybir.dt.bfloat16
I32 = mybir.dt.int32
AF = mybir.ActivationFunctionType
AO = mybir.AluOpType
AX = mybir.AxisListType

D = 2048
T = 2048
TQ = 1024
FI = 8192
R = 256
H = 16
NT = T // 128
NQT = TQ // 128
SC = float(1.0 / np.sqrt(np.float64(128.0)))
NEG = -1.0e30

EXT = [[640 + 128 * j for j in range(4)], [1664 + 128 * j for j in range(4)]]


class P:

    def __init__(self, cur, handle, sems):
        self.cur = cur
        self.h = handle
        self.sems = sems
        self.cnt = {k: 0 for k in sems}

    def op(self, eng, fn, inc=None):
        ins = fn(self.h) if self.cur == eng else None
        if inc is not None:
            s, a = inc
            self.cnt[s] += a
            if ins is not None:
                ins.then_inc(self.sems[s], a)
            return self.cnt[s]
        return None

    def tick(self, sem):
        return self.cnt[sem]

    def wait(self, eng, sem, val=None):
        if val is None:
            val = self.cnt[sem]
        if self.cur == eng and val > 0:
            self.h.wait_ge(self.sems[sem], val)

    def fence(self, engs=("sync", "scalar", "vector", "tensor", "gpsimd")):
        for e in engs:
            for s in self.sems:
                self.wait(e, s)


def ceil_div(a, b):
    return (a + b - 1) // b



def emit_all(p, g):
    emit_consts(p, g)
    if g.dev == "consts":
        p.fence()
        p.op("gpsimd", lambda e: e.dma_start(out=g.dev_out, in_=g.kvwb[:]),
             inc=("st", 16))
        p.wait("gpsimd", "st")
        return
    emit_weight_means(p, g)
    if g.dev == "means":
        p.fence()
        p.op("gpsimd", lambda e: e.dma_start(out=g.dev_out, in_=g.ws3[:]),
             inc=("st", 16))
        p.wait("gpsimd", "st")
        return
    emit_quantize(p, g)
    p.fence()
    if g.dev == "quant":
        for r0 in (0, 4096):
            p.wait("gpsimd", "st")
            p.op("gpsimd", lambda e, r0=r0: e.dma_start(
                out=g.tt(0, F32)[:].bitcast(BF16)[:, 0:2048],
                in_=g.wgq_s[r0:r0 + 128, :]), inc=("sg", 16))
            p.wait("vector", "sg")
            p.op("vector", lambda e: e.tensor_copy(g.tt(1, F32)[:],
                 g.tt(0, F32)[:].bitcast(BF16)[:, 0:2048]), inc=("cv", 1))
            p.wait("gpsimd", "cv")
            p.op("gpsimd", lambda e, r0=r0: e.dma_start(
                out=g.dev_out[(r0 // 4096) * 128:(r0 // 4096) * 128 + 128, :],
                in_=g.tt(1, F32)[:]), inc=("st", 16))
        p.wait("gpsimd", "st")
        return
    if g.dev == "quantd":
        for i, r0 in enumerate((0, 1024)):
            p.wait("gpsimd", "st")
            p.op("gpsimd", lambda e, r0=r0: e.dma_start(
                out=g.pa(BF16)[:, 0:8192],
                in_=g.wdq_s[r0:r0 + 128, :]), inc=("sg", 16))
            p.wait("vector", "sg")
            p.op("vector", lambda e: e.tensor_copy(g.tt(None, F32)[:],
                 g.pa(BF16)[:, 0:8192]), inc=("cv", 1))
            p.wait("gpsimd", "cv")
            p.op("gpsimd", lambda e, i=i: e.dma_start(
                out=g.dev_out[i * 128:(i + 1) * 128, :],
                in_=g.tt(None, F32)[:]), inc=("st", 16))
        p.wait("gpsimd", "st")
        return
    emit_ffn(p, g, app=1)
    p.fence()
    if g.dev in ("ffnA", "h", "wblk"):
        p.wait("gpsimd", "st")
        return
    if g.dev == "x1":
        emit_dev_copy(p, g, g.x1_s, T)
        return
    emit_latent(p, g)
    p.fence()
    if g.dev == "ln1":
        emit_dev_copy(p, g, g.ln1_s, T)
        return
    if g.dev == "latT":
        emit_dev_latT(p, g)
        return
    emit_qproj(p, g)
    p.fence()
    if g.dev == "q":
        emit_dev_copy(p, g, g.q_s, TQ)
        return
    emit_attention(p, g)
    p.fence()
    if g.dev == "attn":
        emit_dev_copy(p, g, g.attn_s, D)
        return
    emit_wo(p, g)
    p.fence()
    if g.dev == "x2":
        emit_dev_copy(p, g, g.x2_s, TQ)
        return
    emit_ffn(p, g, app=2)
    p.fence()
    p.wait("gpsimd", "st")


def emit_dev_copy(p, g, src_dram, rows):
    p.fence()
    ncol = src_dram.shape[1]
    for r0 in range(0, rows, 128):
        p.wait("gpsimd", "sg")
        p.wait("gpsimd", "st")
        p.op("gpsimd", lambda e, r0=r0: e.dma_start(
            out=g.tt(None, F32)[:, 0:ncol], in_=src_dram[r0:r0 + 128, :]),
            inc=("sg", 16))
        p.wait("gpsimd", "sg")
        p.op("gpsimd", lambda e, r0=r0: e.dma_start(
            out=g.dev_out[r0:r0 + 128, :], in_=g.tt(None, F32)[:, 0:ncol]),
            inc=("st", 16))
    p.wait("gpsimd", "st")


def emit_dev_latT(p, g):
    p.fence()
    p.op("gpsimd", lambda e: e.dma_start(out=g.dev_out, in_=g.latT[:]),
         inc=("st", 16))
    p.wait("gpsimd", "st")


def emit_consts(p, g):
    p.op("gpsimd", lambda e: e.memset(g.ident[:], 0.0), inc=("cg", 1))
    p.wait("gpsimd", "cg")
    p.op("gpsimd", lambda e: e.affine_select(
        out=g.ident[:], in_=g.ident[:], compare_op=AO.not_equal,
        fill=1.0, base=0, pattern=[[-1, 128]], channel_multiplier=1), inc=("cg", 1))
    p.op("gpsimd", lambda e: e.memset(g.identb[:], 0.0), inc=("cg", 1))
    p.wait("gpsimd", "cg")
    p.op("gpsimd", lambda e: e.affine_select(
        out=g.identb[:], in_=g.identb[:], compare_op=AO.not_equal,
        fill=1.0, base=0, pattern=[[-1, 128]], channel_multiplier=1), inc=("cg", 1))
    p.op("sync", lambda e: e.dma_start(out=g.kvwb[0:1, :], in_=g.kvw_in),
         inc=("sd", 16))
    p.op("sync", lambda e: e.dma_start(out=g.qidx_t[:],
         in_=g.qidx_in.rearrange("(a p) o -> p (a o)", p=128),
         allow_slow_non_contiguous=True), inc=("sd", 16))
    p.op("vector", lambda e: e.memset(g.onesr[:], 1.0), inc=("cv", 1))
    p.wait("tensor", "sd")
    p.wait("tensor", "cv")
    p.op("tensor", lambda e: e.matmul(g.psA[:, 0:256], g.onesr[:], g.kvwb[0:1, :],
         start=True, stop=True), inc=("ct", 1))
    p.wait("vector", "ct")
    p.op("vector", lambda e: e.tensor_copy(g.kvwb[:], g.psA[:, 0:256]),
         inc=("cv", 1))
    p.op("vector", lambda e: e.memset(g.eps5[:], 1.0e-5))
    p.op("vector", lambda e: e.memset(g.eps6[:], 1.0e-6))
    p.op("vector", lambda e: e.memset(g.zero1[:], 0.0))
    p.op("vector", lambda e: e.memset(g.ws3[:, 7:8], 0.0))
    p.op("vector", lambda e: e.memset(g.ones[:], 1.0))
    p.op("vector", lambda e: e.memset(g.stat[:, 0:1], 127.0), inc=("cv", 1))
    p.wait("vector", "cv")
    p.op("vector", lambda e: e.reciprocal(g.c127r[:], g.stat[:, 0:1]),
         inc=("cv", 1))
    p.wait("vector", "cv")


def emit_weight_means(p, g):
    CH = 4096
    NCH = 131072 // CH
    cb_t = [0, 0]
    buf_t = [0, 0]
    for wi, w_ap in enumerate((g.wg_in, g.wu_in, g.wd_in)):
        flat = w_ap.rearrange("(q a) f -> q (a f)", q=128)
        for k in range(NCH):
            half = k % 2
            buf = g.pa(F32)[:, half * CH:(half + 1) * CH]
            cb = g.pb(F32)[:, 0:CH + 1] if half == 0 else \
                g.pb(F32)[:, CH + 1:2 * CH + 2]
            sem = "sa" if half == 0 else "sb"
            slot = k // (NCH // 2)
            acol = g.msum[:, wi * 2 + slot:wi * 2 + slot + 1]
            if buf_t[half]:
                p.wait("sync", "cs", buf_t[half])
            lt = p.op("sync", lambda e, b=buf, sp=flat[:, k * CH:(k + 1) * CH]:
                      e.dma_start(out=b, in_=sp), inc=(sem, 16))
            p.wait("scalar", sem, lt)
            if cb_t[half]:
                p.wait("scalar", "cv", cb_t[half])
            buf_t[half] = p.op("scalar", lambda e, b=buf, cb=cb: e.activation(
                out=cb[:, 1:CH + 1], in_=b, func=AF.Abs), inc=("cs", 1))
            p.wait("vector", "cs")
            p.wait("vector", "cv")
            if k % (NCH // 2) == 0:
                p.op("vector", lambda e, cb=cb: e.memset(cb[:, 0:1], 0.0),
                     inc=("cv", 1))
            else:
                p.op("vector", lambda e, cb=cb, a=acol: e.tensor_copy(
                    cb[:, 0:1], a), inc=("cv", 1))
            p.wait("vector", "cv")
            cb_t[half] = p.op("vector", lambda e, cb=cb, a=acol: e.reduce_sum(
                a, cb[:, 0:CH + 1], axis=AX.X), inc=("cv", 1))
    p.wait("tensor", "cv")
    p.wait("tensor", "cg")
    for wi in range(3):
        for j in range(2):
            ps = g.psA[0:1, wi * 256 + j * 128:wi * 256 + (j + 1) * 128]
            p.op("tensor", lambda e, wi=wi, j=j, ps=ps: e.transpose(
                ps, g.msum[:, wi * 2 + j:wi * 2 + j + 1], g.ident[:]),
                inc=("ct", 1))
    p.wait("vector", "ct")
    row = g.tt(None, F32)[0:1, 0:256]
    ga = g.tt(None, F32)[0:1, 256:320]
    gb = g.tt(None, F32)[0:1, 320:384]
    _r1 = lambda ap: ap.rearrange("p (a t) -> p a t", t=1)
    for wi in range(3):
        rv = row.rearrange("p (a t) -> p a t", t=2)
        for j in range(2):
            ps = g.psA[0:1, wi * 256 + j * 128:wi * 256 + (j + 1) * 128]
            p.op("vector", lambda e, ps=ps, rv=rv, j=j: e.tensor_copy(
                rv[:, :, j:j + 1], _r1(ps)), inc=("cv", 1))
        p.wait("vector", "cv")
        r4 = row.rearrange("p (a t) -> p a t", t=4)
        p.op("vector", lambda e, r4=r4: e.tensor_tensor(
            out=_r1(ga), in0=r4[:, :, 0:1], in1=r4[:, :, 1:2], op=AO.add),
            inc=("cv", 1))
        p.wait("vector", "cv")
        p.op("vector", lambda e, r4=r4: e.tensor_tensor(
            out=_r1(gb), in0=_r1(ga), in1=r4[:, :, 2:3], op=AO.add),
            inc=("cv", 1))
        p.wait("vector", "cv")
        p.op("vector", lambda e, r4=r4: e.tensor_tensor(
            out=_r1(ga), in0=_r1(gb), in1=r4[:, :, 3:4], op=AO.add),
            inc=("cv", 1))
        p.wait("vector", "cv")
        p.op("vector", lambda e, wi=wi: e.reduce_sum(
            g.stat[0:1, 1 + wi:2 + wi], ga, axis=AX.X), inc=("cv", 1))
        p.wait("vector", "cv")
    p.op("vector", lambda e: e.tensor_scalar(
        out=g.stat[0:1, 8:11], in0=g.stat[0:1, 1:4], scalar1=float(2.0 ** -24),
        scalar2=1.0e-5, op0=AO.mult, op1=AO.add), inc=("cv", 1))
    p.wait("vector", "cv")
    p.op("vector", lambda e: e.reciprocal(g.stat[0:1, 11:14], g.stat[0:1, 8:11]),
         inc=("cv", 1))
    p.wait("tensor", "cv")
    p.op("tensor", lambda e: e.matmul(g.psA[:, 256:259], g.onesr[:],
         g.stat[0:1, 11:14], start=True, stop=True))
    p.op("tensor", lambda e: e.matmul(g.psA[:, 259:262], g.onesr[:],
         g.stat[0:1, 8:11], start=True, stop=True), inc=("ct", 1))
    p.wait("vector", "ct")
    p.op("vector", lambda e: e.tensor_copy(g.ws3[:, 0:3], g.psA[:, 256:259]))
    p.op("vector", lambda e: e.tensor_copy(g.ws3[:, 3:6], g.psA[:, 259:262]),
         inc=("cv", 1))
    p.wait("vector", "cv")
    p.op("vector", lambda e: e.tensor_scalar(
        out=g.ws3[:, 6:7], in0=g.ws3[:, 5:6], scalar1=0.5, scalar2=None,
        op0=AO.mult))
    p.op("vector", lambda e: e.tensor_scalar(
        out=g.ws3[:, 7:8], in0=g.ws3[:, 5:6], scalar1=-0.5, scalar2=None,
        op0=AO.mult), inc=("cv", 1))
    p.wait("vector", "cv")
    p.wait("sync", "cs")
    p.wait("sync", "cv")
    p.wait("scalar", "cv")
    p.wait("tensor", "cv")
    p.wait("scalar", "cv")
    p.wait("sync", "cv")
    p.wait("tensor", "cv")


def emit_quantize(p, g):
    jobs = []
    for w_ap, dst, nrow, ncol, wi in ((g.wg_in, g.wgq_s, FI, D, 0),
                                      (g.wu_in, g.wuq_s, FI, D, 1),
                                      (g.wd_in, g.wdq_s, D, FI, 2)):
        rps = 128 * (8192 // ncol)
        for s in range(nrow // rps):
            jobs.append((w_ap, dst, rps, s, wi))
    wqb = g.pc(BF16)[:, 0:8192]
    ri = g.pb(I32)[:, 8192:16384]
    junk = g.pb(F32)[:, 0:8192]
    zf2 = g.pb(F32)[:, 8192:16384]
    store_tick = [0]
    freed = []
    for k, (w_ap, dst, rps, s, wi) in enumerate(jobs):
        buf = g.pa(F32)[:, 0:8192] if k % 2 == 0 else g.pa(F32)[:, 8192:16384]
        sem = "sa" if k % 2 == 0 else "sb"
        if k >= 2:
            fs, ft = freed[k - 2]
            p.wait("sync", fs, ft)
        src = w_ap[s * rps:(s + 1) * rps, :].rearrange("(q a) f -> q (a f)", q=128)
        lt = p.op("sync", lambda e, b=buf, sp=src: e.dma_start(out=b, in_=sp),
                  inc=(sem, 16))
        p.wait("scalar", sem, lt)
        p.wait("scalar", "cv")
        if wi < 2:
            p.op("scalar", lambda e, b=buf, wi=wi: e.activation(
                out=ri, in_=b, func=AF.Identity, scale=g.ws3[:, wi:wi + 1]),
                inc=("cs", 1))
            freed.append(("cs", p.tick("cs")))
            p.wait("vector", "cs")
            p.wait("vector", "sg", store_tick[0])
            p.op("vector", lambda e: e.tensor_scalar(
                out=wqb, in0=ri, scalar1=1.0, scalar2=-1.0, op0=AO.min,
                op1=AO.max), inc=("cv", 1))
        else:
            p.op("scalar", lambda e, b=buf: e.activation(
                out=junk, in_=b, func=AF.Abs), inc=("cs", 1))
            p.wait("scalar", "cs")
            p.op("scalar", lambda e: e.activation(
                out=zf2, in_=junk, func=AF.Relu, bias=g.ws3[:, 7:8]),
                inc=("cs", 1))
            p.wait("vector", "cs")
            p.op("vector", lambda e, b=buf: e.scalar_tensor_tensor(
                out=junk, in0=zf2, scalar=1.0e30, in1=b, op0=AO.mult,
                op1=AO.mult), inc=("cv", 1))
            freed.append(("cv", p.tick("cv")))
            p.wait("vector", "cv")
            p.wait("vector", "sg", store_tick[0])
            p.op("vector", lambda e: e.tensor_scalar(
                out=wqb, in0=junk, scalar1=1.0, scalar2=-1.0, op0=AO.min,
                op1=AO.max), inc=("cv", 1))
        p.wait("gpsimd", "cv")
        dstap = dst[s * rps:(s + 1) * rps, :].rearrange("(q a) f -> q (a f)", q=128)
        store_tick[0] = p.op("gpsimd", lambda e, dp=dstap: e.dma_start(
            out=dp, in_=wqb), inc=("sg", 16))
    p.wait("sync", "sg")
    p.wait("sync", "cs")
    p.wait("sync", "cv")
    tb0 = g.pa(BF16)[:, 0:2048]
    tb1 = g.pa(BF16)[:, 2048:4096]
    st_tick = [0, 0]
    for c in range(FI // 128):
        tb = tb0 if c % 2 == 0 else tb1
        rsem = "sa" if c % 2 == 0 else "sb"
        ssem = "sc" if c % 2 == 0 else "se"
        if c >= 2:
            p.wait("sync", ssem, st_tick[c % 2])
        rt = p.op("sync", lambda e, c=c, tb=tb: e.dma_start(
            out=tb, in_=g.wdq_s[:, c * 128:(c + 1) * 128], transpose=True),
            inc=(rsem, 16))
        p.wait("scalar", rsem, rt)
        st_tick[c % 2] = p.op("scalar", lambda e, c=c, tb=tb: e.dma_start(
            out=g.wdqT_s[c * 128:(c + 1) * 128, :], in_=tb), inc=(ssem, 16))
    p.wait("scalar", "sc")
    p.wait("scalar", "se")


def emit_ffn(p, g, app):
    Tt = T if app == 1 else TQ
    nt = Tt // 128
    src = g.x_in if app == 1 else g.x2_s
    dst = g.x1_s if app == 1 else g.out

    xqT = g.pa(BF16)
    for t in range(nt):
        emit_ln_quant_tile(p, g, src, t, nt, xqT)

    p.fence()
    if g.dev == "ffnA" and app == 1:
        p.op("vector", lambda e: e.tensor_copy(g.tt(0, F32)[:],
             xqT[:, 0:2048]), inc=("cv", 1))
        p.wait("gpsimd", "cv")
        p.op("gpsimd", lambda e: e.dma_start(out=g.dev_out, in_=g.tt(0, F32)[:]),
             inc=("st", 16))
        p.wait("gpsimd", "st")
        return

    emit_gate_up(p, g, app, nt, xqT)

    p.fence()
    if g.dev == "h" and app == 1:
        for i, r0 in enumerate((0, 1024)):
            p.wait("gpsimd", "st")
            p.wait("gpsimd", "cv", p.tick("cv"))
            p.op("gpsimd", lambda e, r0=r0: e.dma_start(
                out=g.tt(None, F32)[:, 0:8192], in_=g.h_s[r0:r0 + 128, :]),
                inc=("sg", 16))
            p.wait("gpsimd", "sg")
            p.op("gpsimd", lambda e, i=i: e.dma_start(
                out=g.dev_out[i * 128:(i + 1) * 128, :],
                in_=g.tt(None, F32)[:, 0:8192]), inc=("st", 16))
        p.wait("gpsimd", "st")
        return
    for m0 in range(0, nt, 4):
        emit_down_macro(p, g, app, nt, m0, src, dst)
    p.fence()


def emit_ln_quant_tile(p, g, src, t, nt, xqT):
    xt = g.tt(0, F32)
    lnt = g.tt(1, F32)
    sq = g.tt(2, F32)
    ri = g.pb(I32)[:, 0:2048]
    xqb = g.pb(BF16)[:, 8192:10240]
    st = g.stat
    p.wait("sync", "cv", p.tick("cv"))
    p.wait("sync", "cs", p.tick("cs"))
    p.op("sync", lambda e: e.dma_start(out=xt[:], in_=src[t * 128:(t + 1) * 128, :]),
         inc=("sd", 16))
    p.wait("vector", "sd")
    p.op("vector", lambda e: e.reduce_sum(st[:, 0:1], xt[:], axis=AX.X),
         inc=("cv", 1))
    p.wait("scalar", "sd")
    p.op("scalar", lambda e: e.activation(out=sq[:], in_=xt[:], func=AF.Square,
         accum_out=st[:, 1:2]), inc=("cs", 1))
    p.wait("vector", "cv")
    p.wait("vector", "cs")
    p.op("vector", lambda e: e.tensor_scalar(
        out=st[:, 2:3], in0=st[:, 0:1], scalar1=float(1.0 / D), scalar2=None,
        op0=AO.mult))
    p.op("vector", lambda e: e.tensor_scalar(
        out=st[:, 3:4], in0=st[:, 1:2], scalar1=float(1.0 / D), scalar2=None,
        op0=AO.mult), inc=("cv", 1))
    p.wait("vector", "cv")
    p.op("vector", lambda e: e.tensor_tensor(
        out=st[:, 4:5], in0=st[:, 2:3], in1=st[:, 2:3], op=AO.mult),
        inc=("cv", 1))
    p.wait("vector", "cv")
    p.op("vector", lambda e: e.tensor_sub(st[:, 5:6], st[:, 3:4], st[:, 4:5]),
         inc=("cv", 1))
    p.wait("scalar", "cv")
    p.op("scalar", lambda e: e.activation(out=st[:, 6:7], in_=st[:, 5:6],
         func=AF.Sqrt, bias=g.eps5[:]), inc=("cs", 1))
    p.wait("vector", "cs")
    p.op("vector", lambda e: e.reciprocal(st[:, 7:8], st[:, 6:7]), inc=("cv", 1))
    p.wait("vector", "cv")
    p.op("vector", lambda e: e.scalar_tensor_tensor(
        out=st[:, 8:9], in0=st[:, 2:3], scalar=-1.0, in1=st[:, 7:8],
        op0=AO.mult, op1=AO.mult), inc=("cv", 1))
    p.wait("scalar", "cv")
    p.op("scalar", lambda e: e.activation(out=lnt[:], in_=xt[:], func=AF.Identity,
         scale=st[:, 7:8], bias=st[:, 8:9]), inc=("cs", 1))
    p.wait("vector", "cs")
    p.op("vector", lambda e: e.reduce_max(st[:, 9:10], lnt[:], axis=AX.X,
         apply_absolute_value=True), inc=("cv", 1))
    p.wait("vector", "cv")
    p.op("vector", lambda e: e.tensor_scalar(
        out=st[:, 10:11], in0=st[:, 9:10], scalar1=g.c127r[:], scalar2=g.eps5[:],
        op0=AO.mult, op1=AO.add), inc=("cv", 1))
    p.wait("vector", "cv")
    p.op("vector", lambda e: e.reciprocal(st[:, 11:12], st[:, 10:11]),
         inc=("cv", 1))
    p.wait("vector", "cv")
    p.op("vector", lambda e: e.tensor_tensor(
        out=g.xscol[:, t:t + 1], in0=st[:, 10:11], in1=g.ws3[:, 3:4], op=AO.mult))
    p.op("vector", lambda e: e.tensor_tensor(
        out=g.xscol[:, 16 + t:17 + t], in0=st[:, 10:11], in1=g.ws3[:, 4:5],
        op=AO.mult), inc=("cv", 1))
    p.wait("scalar", "cv")
    p.op("scalar", lambda e: e.activation(out=sq[:], in_=lnt[:], func=AF.Identity,
         scale=st[:, 11:12]), inc=("cs", 1))
    p.wait("vector", "cs")
    p.op("vector", lambda e: e.tensor_copy(ri, sq[:]), inc=("cv", 1))
    p.wait("vector", "cv")
    p.op("vector", lambda e: e.tensor_copy(xqb, ri), inc=("cv", 1))
    p.wait("tensor", "cv")
    for c in range(16):
        ps = (g.psB[:, 0:64] if c % 2 == 0 else g.psB[:, 512:576]).bitcast(BF16)
        if c >= 2:
            p.wait("tensor", "cv", p.tick("cv") - 1)
        p.op("tensor", lambda e, c=c, ps=ps: e.transpose(
            ps, xqb[:, c * 128:(c + 1) * 128], g.identb[:]),
            inc=("ct", 1))
        p.wait("vector", "ct")
        p.op("vector", lambda e, c=c, ps=ps: e.tensor_copy(
            xqT[:, c * (nt * 128) + t * 128: c * (nt * 128) + (t + 1) * 128],
            ps), inc=("cv", 1))
    p.wait("tensor", "cv")


def emit_gate_up(p, g, app, nt, xqT):
    wbuf = [g.pb(BF16)[:, 0:16384], g.pb(BF16)[:, 16384:32768]]
    silu_t = g.tt(0, F32)[:, 0:512]
    hbuf = [g.tt(1, F32)[:, 0:512], g.tt(1, F32)[:, 512:1024]]
    nob = 16

    def load_block(ob, which):
        b = wbuf[ob % 2]
        sem = "sa" if ob % 2 == 0 else "sb"
        lt = None
        for c in range(16):
            for mi, scr in enumerate((g.wgq_s, g.wuq_s)):
                lt = p.op(which, lambda e, b=b, c=c, mi=mi, scr=scr, ob=ob:
                          e.dma_start(
                    out=b[:, mi * 8192 + c * 512:mi * 8192 + (c + 1) * 512],
                    in_=scr[ob * 512:(ob + 1) * 512, c * 128:(c + 1) * 128],
                    transpose=True),
                    inc=(sem, 16))
        return sem, lt

    blk_tick = [None, None]
    blk_tick[0] = load_block(0, "sync")
    hdma_tick = [0, 0]
    if g.dev == "wblk" and app == 1:
        blk_tick[1] = load_block(1, "scalar")
        p.wait("vector", *blk_tick[0])
        p.wait("vector", *blk_tick[1])
        p.op("vector", lambda e: e.tensor_copy(g.tt(0, F32)[:, 0:512],
             wbuf[0][:, 0:512]))
        p.op("vector", lambda e: e.tensor_copy(g.tt(0, F32)[:, 512:1024],
             wbuf[0][:, 8192:8704]))
        p.op("vector", lambda e: e.tensor_copy(g.tt(0, F32)[:, 1024:1536],
             wbuf[1][:, 0:512]))
        p.op("vector", lambda e: e.tensor_copy(g.tt(0, F32)[:, 1536:2048],
             wbuf[1][:, 8192:8704]), inc=("cv", 1))
        p.wait("gpsimd", "cv")
        p.op("gpsimd", lambda e: e.dma_start(out=g.dev_out, in_=g.tt(0, F32)[:]),
             inc=("st", 16))
        p.wait("gpsimd", "st")
        return
    for ob in range(nob):
        b = wbuf[ob % 2]
        p.wait("tensor", *blk_tick[ob % 2])
        for t in range(nt):
            pg = g.psA[:, 0:512] if t % 2 == 0 else g.psA[:, 1024:1536]
            pu = g.psA[:, 512:1024] if t % 2 == 0 else g.psA[:, 1536:2048]
            p.wait("tensor", "cv", p.tick("cv") - 1)
            p.wait("tensor", "cs", p.tick("cs") - 1)
            tg = tu = None
            for c in range(16):
                tg = p.op("tensor", lambda e, b=b, c=c, t=t, pg=pg: e.matmul(
                    pg, xqT[:, c * (nt * 128) + t * 128:c * (nt * 128) + (t + 1) * 128],
                    b[:, c * 512:(c + 1) * 512],
                    start=(c == 0), stop=(c == 15)),
                    inc=("ct", 1) if c == 15 else None)
            for c in range(16):
                tu = p.op("tensor", lambda e, b=b, c=c, t=t, pu=pu: e.matmul(
                    pu, xqT[:, c * (nt * 128) + t * 128:c * (nt * 128) + (t + 1) * 128],
                    b[:, 8192 + c * 512:8192 + (c + 1) * 512],
                    start=(c == 0), stop=(c == 15)),
                    inc=("ct", 1) if c == 15 else None)
            p.wait("scalar", "ct", tg)
            p.wait("scalar", "cv")
            p.op("scalar", lambda e, t=t, pg=pg: e.activation(
                out=silu_t, in_=pg, func=AF.Silu, scale=g.xscol[:, t:t + 1]),
                inc=("cs", 1))
            p.wait("vector", "ct", tu)
            p.wait("vector", "cs")
            hb = hbuf[t % 2]
            hsem = "ha" if t % 2 == 0 else "hb"
            p.wait("vector", hsem, hdma_tick[t % 2])
            p.op("vector", lambda e, t=t, pu=pu, hb=hb: e.scalar_tensor_tensor(
                out=hb, in0=pu, scalar=g.xscol[:, 16 + t:17 + t], in1=silu_t,
                op0=AO.mult, op1=AO.mult), inc=("cv", 1))
            p.wait("vector", "cv")
            p.op("vector", lambda e, t=t, ob=ob, hb=hb: e.reduce_max(
                g.hmx[:, t * 16 + ob:t * 16 + ob + 1], hb, axis=AX.X,
                apply_absolute_value=True), inc=("cv", 1))
            p.wait("gpsimd", "cv")
            hdma_tick[t % 2] = p.op("gpsimd", lambda e, t=t, ob=ob, hb=hb:
                e.dma_start(
                    out=g.h_s[t * 128:(t + 1) * 128, ob * 512:(ob + 1) * 512],
                    in_=hb), inc=(hsem, 16))
            if t == 0 and ob + 1 < nob:
                q = "sync" if (ob + 1) % 2 == 0 else "scalar"
                p.wait(q, "ct", p.tick("ct"))
                blk_tick[(ob + 1) % 2] = load_block(ob + 1, q)
    p.fence()


def emit_down_macro(p, g, app, nt, m0, src, dst):
    ntm = min(4, nt - m0)
    hq_T = g.pa(BF16)
    yacc = g.pc(F32)[:, 0:8192]
    hbuf = g.tt(None, F32)
    hsc = g.pb(F32)[:, 0:8192]
    ri = g.pb(I32)[:, 8192:16384]
    hqb = g.pb(BF16)[:, 0:8192]
    st = g.stat

    for ti in range(ntm):
        t = m0 + ti
        p.wait("sync", "cv", p.tick("cv"))
        p.wait("sync", "cs", p.tick("cs"))
        p.wait("sync", "ct", p.tick("ct"))
        lt = p.op("sync", lambda e, t=t: e.dma_start(
            out=hbuf[:], in_=g.h_s[t * 128:(t + 1) * 128, :]), inc=("sd", 16))
        p.wait("vector", "sd", lt)
        p.wait("scalar", "sd", lt)
        p.op("vector", lambda e, t=t: e.reduce_max(
            st[:, 0:1], g.hmx[:, t * 16:t * 16 + 16], axis=AX.X), inc=("cv", 1))
        p.wait("vector", "cv")
        p.op("vector", lambda e: e.tensor_scalar(
            out=st[:, 1:2], in0=st[:, 0:1], scalar1=g.c127r[:], scalar2=g.eps5[:],
            op0=AO.mult, op1=AO.add), inc=("cv", 1))
        p.wait("vector", "cv")
        p.op("vector", lambda e: e.reciprocal(st[:, 2:3], st[:, 1:2]), inc=("cv", 1))
        p.wait("vector", "cv")
        p.op("vector", lambda e, t=t: e.scalar_tensor_tensor(
            out=g.xscol[:, 32 + (t % 16):33 + (t % 16)], in0=st[:, 1:2], scalar=0.5,
            in1=g.ws3[:, 5:6], op0=AO.mult, op1=AO.mult), inc=("cv", 1))
        p.wait("scalar", "cv")
        p.op("scalar", lambda e: e.activation(out=hsc, in_=hbuf[:],
             func=AF.Identity, scale=st[:, 2:3]), inc=("cs", 1))
        p.wait("vector", "cs")
        p.op("vector", lambda e: e.tensor_copy(ri, hsc), inc=("cv", 1))
        p.wait("vector", "cv")
        p.op("vector", lambda e: e.tensor_copy(hqb, ri), inc=("cv", 1))
        p.wait("tensor", "cv")
        for c in range(64):
            ps = (g.psB[:, 0:64] if c % 2 == 0 else g.psB[:, 512:576]).bitcast(BF16)
            if c >= 2:
                p.wait("tensor", "cv", p.tick("cv") - 1)
            p.op("tensor", lambda e, c=c, ps=ps: e.transpose(
                ps, hqb[:, c * 128:(c + 1) * 128],
                g.identb[:]), inc=("ct", 1))
            p.wait("vector", "ct")
            p.op("vector", lambda e, c=c, ti=ti, ps=ps: e.tensor_copy(
                hq_T[:, ti * 8192 + c * 128:ti * 8192 + (c + 1) * 128],
                ps), inc=("cv", 1))
    p.fence()

    wdb = [g.pb(BF16)[:, 0:16384], g.pb(BF16)[:, 16384:32768]]

    def load_half(hb, which):
        b = wdb[hb % 2]
        sem = "sa" if hb % 2 == 0 else "sb"
        lt = p.op(which, lambda e, b=b, hb=hb: e.dma_start(
            out=b.rearrange("q (c f) -> q c f", c=8),
            in_=g.wdqT_s[hb * 1024:(hb + 1) * 1024, :]
            .rearrange("(c q) f -> q c f", q=128)),
            inc=(sem, 16))
        return sem, lt

    half_tick = [None, None]
    half_tick[0] = load_half(0, "sync")
    for hb in range(8):
        b = wdb[hb % 2]
        p.wait("tensor", *half_tick[hb % 2])
        if hb + 1 < 8:
            q = "sync" if (hb + 1) % 2 == 0 else "scalar"
            p.wait(q, "ct", p.tick("ct"))
            half_tick[(hb + 1) % 2] = load_half(hb + 1, q)
        for ti in range(ntm):
            pm = g.psA if ti % 2 == 0 else g.psB
            p.wait("tensor", "cv", p.tick("cv") - 1)
            for i in range(8):
                ic = hb * 8 + i
                for hgi in range(4):
                    p.op("tensor", lambda e, b=b, i=i, ic=ic, ti=ti, hgi=hgi,
                         pm=pm: e.matmul(
                             pm[:, hgi * 512:(hgi + 1) * 512],
                             hq_T[:, ti * 8192 + ic * 128:ti * 8192 + (ic + 1) * 128],
                             b[:, i * 2048 + hgi * 512:i * 2048 + (hgi + 1) * 512],
                             start=(i == 0), stop=(i == 7)),
                         inc=("ct", 1) if (i == 7 and hgi == 3) else None)
            p.wait("vector", "ct")
            if hb == 0:
                p.op("vector", lambda e, ti=ti, pm=pm: e.tensor_copy(
                    yacc[:, ti * 2048:(ti + 1) * 2048], pm[:, 0:2048]),
                    inc=("cv", 1))
            else:
                p.op("vector", lambda e, ti=ti, pm=pm: e.scalar_tensor_tensor(
                    out=yacc[:, ti * 2048:(ti + 1) * 2048], in0=pm[:, 0:2048],
                    scalar=1.0, in1=yacc[:, ti * 2048:(ti + 1) * 2048],
                    op0=AO.mult, op1=AO.add), inc=("cv", 1))
    p.fence()

    osem = "st" if dst is g.out else "sg"
    for ti in range(ntm):
        t = m0 + ti
        xt = g.tt(0, F32)
        p.wait("sync", "cv", p.tick("cv"))
        lt = p.op("sync", lambda e, t=t: e.dma_start(
            out=xt[:], in_=src[t * 128:(t + 1) * 128, :]), inc=("sd", 16))
        p.wait("vector", "sd", lt)
        p.wait("vector", osem)
        p.op("vector", lambda e, t=t, ti=ti: e.scalar_tensor_tensor(
            out=g.tt(1, F32)[:], in0=yacc[:, ti * 2048:(ti + 1) * 2048],
            scalar=g.xscol[:, 32 + (t % 16):33 + (t % 16)], in1=xt[:],
            op0=AO.mult, op1=AO.add), inc=("cv", 1))
        p.wait("gpsimd", "cv")
        p.op("gpsimd", lambda e, t=t: e.dma_start(
            out=dst[t * 128:(t + 1) * 128, :], in_=g.tt(1, F32)[:]),
            inc=(osem, 16))
    p.fence()


def emit_latent(p, g):
    wkvd = g.pa(F32)[:, 0:4096]
    p.op("sync", lambda e: e.dma_start(
        out=wkvd.rearrange("q (c r) -> q c r", c=16),
        in_=g.wkvd_in.rearrange("(c q) r -> q c r", q=128)),
        inc=("sd", 16))
    lnT = g.tt(3, F32)
    for t in range(NT):
        xt = g.tt(0, F32)
        lnt = g.tt(1, F32)
        sq = g.tt(2, F32)
        st = g.stat
        p.wait("sync", "cv", p.tick("cv"))
        p.wait("sync", "cs", p.tick("cs"))
        p.wait("sync", "ct", p.tick("ct"))
        p.wait("sync", "sg", p.tick("sg"))
        p.op("sync", lambda e, t=t: e.dma_start(
            out=xt[:], in_=g.x1_s[t * 128:(t + 1) * 128, :]), inc=("sd", 16))
        p.wait("vector", "sd")
        p.op("vector", lambda e: e.reduce_sum(st[:, 0:1], xt[:], axis=AX.X),
             inc=("cv", 1))
        p.wait("scalar", "sd")
        p.op("scalar", lambda e: e.activation(out=sq[:], in_=xt[:], func=AF.Square,
             accum_out=st[:, 1:2]), inc=("cs", 1))
        p.wait("vector", "cv")
        p.wait("vector", "cs")
        p.op("vector", lambda e: e.tensor_scalar(
            out=st[:, 2:3], in0=st[:, 0:1], scalar1=float(1.0 / D), scalar2=None,
            op0=AO.mult))
        p.op("vector", lambda e: e.tensor_scalar(
            out=st[:, 3:4], in0=st[:, 1:2], scalar1=float(1.0 / D), scalar2=None,
            op0=AO.mult), inc=("cv", 1))
        p.wait("vector", "cv")
        p.op("vector", lambda e: e.tensor_tensor(
            out=st[:, 4:5], in0=st[:, 2:3], in1=st[:, 2:3], op=AO.mult),
            inc=("cv", 1))
        p.wait("vector", "cv")
        p.op("vector", lambda e: e.tensor_sub(st[:, 5:6], st[:, 3:4], st[:, 4:5]),
             inc=("cv", 1))
        p.wait("scalar", "cv")
        p.op("scalar", lambda e: e.activation(out=st[:, 6:7], in_=st[:, 5:6],
             func=AF.Sqrt, bias=g.eps5[:]), inc=("cs", 1))
        p.wait("vector", "cs")
        p.op("vector", lambda e: e.reciprocal(st[:, 7:8], st[:, 6:7]), inc=("cv", 1))
        p.wait("vector", "cv")
        p.op("vector", lambda e: e.scalar_tensor_tensor(
            out=st[:, 8:9], in0=st[:, 2:3], scalar=-1.0, in1=st[:, 7:8],
            op0=AO.mult, op1=AO.mult), inc=("cv", 1))
        p.wait("scalar", "cv")
        p.op("scalar", lambda e: e.activation(out=lnt[:], in_=xt[:],
             func=AF.Identity, scale=st[:, 7:8], bias=st[:, 8:9]), inc=("cs", 1))
        p.wait("gpsimd", "cs")
        p.op("gpsimd", lambda e, t=t: e.dma_start(
            out=g.ln1_s[t * 128:(t + 1) * 128, :], in_=lnt[:]), inc=("sg", 16))
        p.wait("tensor", "cs")
        for c in range(16):
            ps = g.psB[:, 0:128] if c % 2 == 0 else g.psB[:, 512:640]
            if c >= 2:
                p.wait("tensor", "cv", p.tick("cv") - 1)
            p.op("tensor", lambda e, c=c, ps=ps: e.transpose(
                ps, lnt[:, c * 128:(c + 1) * 128], g.ident[:]), inc=("ct", 1))
            p.wait("vector", "ct")
            p.op("vector", lambda e, c=c, ps=ps: e.tensor_copy(
                lnT[:, c * 128:(c + 1) * 128], ps), inc=("cv", 1))
        p.wait("tensor", "cv")
        p.wait("tensor", "cs")
        p.wait("tensor", "sd")
        for c in range(16):
            p.op("tensor", lambda e, c=c: e.matmul(
                g.psA[:, 0:256], lnT[:, c * 128:(c + 1) * 128],
                wkvd[:, c * 256:(c + 1) * 256], start=(c == 0), stop=(c == 15)),
                inc=("ct", 1) if c == 15 else None)
        p.wait("scalar", "ct")
        p.op("scalar", lambda e: e.activation(out=g.tt(2, F32)[:, 0:256],
             in_=g.psA[:, 0:256], func=AF.Square, accum_out=st[:, 9:10]),
             inc=("cs", 1))
        p.wait("vector", "cs")
        p.op("vector", lambda e: e.tensor_scalar(
            out=st[:, 10:11], in0=st[:, 9:10], scalar1=float(1.0 / R),
            scalar2=g.eps6[:], op0=AO.mult, op1=AO.add), inc=("cv", 1))
        p.wait("scalar", "cv")
        p.op("scalar", lambda e: e.activation(out=st[:, 12:13], in_=st[:, 10:11],
             func=AF.Sqrt), inc=("cs", 1))
        p.wait("vector", "cs")
        p.op("vector", lambda e: e.reciprocal(st[:, 13:14], st[:, 12:13]),
             inc=("cv", 1))
        p.wait("scalar", "cv")
        p.op("scalar", lambda e: e.activation(out=g.tt(2, F32)[:, 0:256],
             in_=g.psA[:, 0:256], func=AF.Identity, scale=st[:, 13:14]),
             inc=("cs", 1))
        p.wait("vector", "cs")
        p.op("vector", lambda e: e.tensor_tensor(
            out=g.tt(2, F32)[:, 256:512], in0=g.tt(2, F32)[:, 0:256],
            in1=g.kvwb[:], op=AO.mult), inc=("cv", 1))
        p.wait("tensor", "cv")
        for c in range(2):
            ps = g.psB[:, 256:384]
            p.op("tensor", lambda e, c=c, ps=ps: e.transpose(
                ps, g.tt(2, F32)[:, 256 + c * 128:256 + (c + 1) * 128], g.ident[:]),
                inc=("ct", 1))
            p.wait("vector", "ct")
            p.op("vector", lambda e, c=c, t=t, ps=ps: e.tensor_copy(
                g.latT[:, c * T + t * 128:c * T + (t + 1) * 128], ps), inc=("cv", 1))
            p.wait("tensor", "cv")


def emit_qproj(p, g):
    lnqT = g.pa(F32)
    for t in range(NQT):
        gt = g.tt(0, F32)
        p.wait("gpsimd", "cv", p.tick("cv"))
        lt = p.op("gpsimd", lambda e, t=t: e.indirect_dma_start(
            out=gt[:], out_offset=None, in_=g.ln1_s[:, :],
            in_offset=IndirectOffsetOnAxis(g.qidx_t[:, t:t + 1], 0)),
            inc=("g1", 16))
        p.wait("tensor", "g1", lt)
        for c in range(16):
            ps = g.psB[:, 0:128] if c % 2 == 0 else g.psB[:, 512:640]
            if c >= 2:
                p.wait("tensor", "cv", p.tick("cv") - 1)
            p.op("tensor", lambda e, c=c, ps=ps: e.transpose(
                ps, gt[:, c * 128:(c + 1) * 128], g.ident[:]), inc=("ct", 1))
            p.wait("vector", "ct")
            p.op("vector", lambda e, c=c, t=t, ps=ps: e.tensor_copy(
                lnqT[:, c * 1024 + t * 128:c * 1024 + (t + 1) * 128], ps),
                inc=("cv", 1))
        p.wait("gpsimd", "cv")
    p.fence()
    wqb = [g.pb(F32)[:, 0:8192], g.pb(F32)[:, 8192:16384]]

    def load_wq(ab, which):
        p.op(which, lambda e, ab=ab: e.dma_start(
            out=wqb[ab % 2].rearrange("q (c f) -> q c f", c=16),
            in_=g.wq_in[:, ab * 512:(ab + 1) * 512]
            .rearrange("(c q) f -> q c f", q=128)),
            inc=("sd" if which == "sync" else "sq", 16))

    load_wq(0, "sync")
    for ab in range(4):
        p.wait("tensor", "sd")
        p.wait("tensor", "sq")
        if ab + 1 < 4:
            q = "scalar" if (ab + 1) % 2 else "sync"
            p.wait(q, "ct", p.tick("ct"))
            load_wq(ab + 1, q)
        for t in range(NQT):
            pm = g.psA[:, 0:512] if t % 2 == 0 else g.psA[:, 512:1024]
            p.wait("tensor", "cv", p.tick("cv") - 1)
            for c in range(16):
                p.op("tensor", lambda e, ab=ab, t=t, c=c, pm=pm: e.matmul(
                    pm, lnqT[:, c * 1024 + t * 128:c * 1024 + (t + 1) * 128],
                    wqb[ab % 2][:, c * 512:(c + 1) * 512],
                    start=(c == 0), stop=(c == 15)),
                    inc=("ct", 1) if c == 15 else None)
            p.wait("vector", "ct")
            p.wait("vector", "sg")
            p.op("vector", lambda e, pm=pm: e.tensor_copy(g.tt(1, F32)[:, 0:512], pm),
                 inc=("cv", 1))
            p.wait("gpsimd", "cv")
            p.op("gpsimd", lambda e, ab=ab, t=t: e.dma_start(
                out=g.q_s[t * 128:(t + 1) * 128, ab * 512:(ab + 1) * 512],
                in_=g.tt(1, F32)[:, 0:512]), inc=("sg", 16))
    p.fence()


def emit_attention(p, g):
    wkvu = g.pa(F32)[:, 0:8192]
    p.op("sync", lambda e: e.dma_start(
        out=wkvu.rearrange("q (c f) -> q c f", c=2),
        in_=g.wkvu_in.rearrange("(c q) f -> q c f", q=128)),
        inc=("sd", 16))
    masks = g.pb(F32)[:, 8192:16384]
    p.op("sync", lambda e: e.dma_start(
        out=masks.rearrange("q (a f) -> q a f", a=8),
        in_=g.mask_in.rearrange("(a q) f -> q a f", q=128)),
        inc=("sd", 16))
    kT = g.tt(0, F32)
    vb = g.tt(1, F32)
    Sb = g.tt(2, F32)
    Pb = g.tt(3, F32)
    PT = g.pb(F32)[:, 0:8192]
    att_st = g.pa(F32)[:, 8704:9216]
    qstg = g.pa(F32)[:, 9216:10240]
    qTh = g.pa(F32)[:, 10240:11264]
    for h in range(H):
        p.wait("tensor", "sd")
        p.wait("tensor", "cv", p.tick("cv"))
        for yb in range(4):
            pm = g.psA[:, yb * 512:(yb + 1) * 512]
            for c in range(2):
                p.op("tensor", lambda e, h=h, yb=yb, c=c, pm=pm: e.matmul(
                    pm, wkvu[:, c * 4096 + h * 128:c * 4096 + (h + 1) * 128],
                    g.latT[:, c * T + yb * 512:c * T + (yb + 1) * 512],
                    start=(c == 0), stop=(c == 1)),
                    inc=("ct", 1) if (yb == 3 and c == 1) else None)
        p.wait("vector", "ct")
        p.op("vector", lambda e: e.tensor_copy(kT[:], g.psA[:, 0:2048]),
             inc=("cv", 1))
        for k in range(16):
            pm = g.psB[:, 0:128] if k % 2 == 0 else g.psB[:, 512:640]
            if k >= 2:
                p.wait("tensor", "cv", p.tick("cv") - 1)
            for c in range(2):
                p.op("tensor", lambda e, h=h, k=k, c=c, pm=pm: e.matmul(
                    pm, g.latT[:, c * T + k * 128:c * T + (k + 1) * 128],
                    wkvu[:, c * 4096 + 2048 + h * 128:c * 4096 + 2048 + (h + 1) * 128],
                    start=(c == 0), stop=(c == 1)),
                    inc=("ct", 1) if c == 1 else None)
            p.wait("vector", "ct")
            p.op("vector", lambda e, k=k, pm=pm: e.tensor_copy(
                vb[:, k * 128:(k + 1) * 128], pm), inc=("cv", 1))
        p.wait("sync", "ct", p.tick("ct"))
        qlt = []
        for u in range(8):
            qlt.append(p.op("sync", lambda e, u=u, h=h: e.dma_start(
                out=qstg[:, u * 128:(u + 1) * 128],
                in_=g.q_s[u * 128:(u + 1) * 128, h * 128:(h + 1) * 128]),
                inc=("sd", 16)))
        for u in range(8):
            ps = g.psB[:, 256:384] if u % 2 == 0 else g.psB[:, 768:896]
            p.wait("tensor", "sd", qlt[7])
            if u >= 2:
                p.wait("tensor", "cv", p.tick("cv") - 1)
            p.op("tensor", lambda e, u=u, ps=ps: e.transpose(
                ps, qstg[:, u * 128:(u + 1) * 128], g.ident[:]), inc=("ct", 1))
            p.wait("vector", "ct")
            p.op("vector", lambda e, u=u, ps=ps: e.tensor_copy(
                qTh[:, u * 128:(u + 1) * 128], ps), inc=("cv", 1))
        for s in range(2):
            smax = EXT[s][3]
            nyt = smax // 128
            p.wait("vector", "ct", p.tick("ct"))
            ragged = [(j, k) for j in range(4) for k in range(EXT[s][j] // 128, nyt)]
            for n, (j, k) in enumerate(ragged):
                p.op("vector", lambda e, k=k, j=j: e.memset(
                    PT[:, k * 512 + j * 128:k * 512 + (j + 1) * 128], 0.0),
                    inc=("cv", 1) if n == len(ragged) - 1 else None)
            for j in range(4):
                E = EXT[s][j]
                xt_global = s * 4 + j
                qTu = qTh[:, xt_global * 128:(xt_global + 1) * 128]
                p.wait("tensor", "cv", p.tick("cv"))
                nblk = ceil_div(E, 512)
                for kb in range(nblk):
                    w = min(512, E - kb * 512)
                    pm = g.psB[:, 0:512] if kb % 2 == 0 else g.psB[:, 512:1024]
                    if kb >= 2:
                        p.wait("tensor", "cv", p.tick("cv") - 1)
                    p.op("tensor", lambda e, kb=kb, w=w, pm=pm, qTu=qTu: e.matmul(
                        pm[:, 0:w], qTu, kT[:, kb * 512:kb * 512 + w],
                        start=True, stop=True), inc=("ct", 1))
                    p.wait("vector", "ct")
                    mstart = E - 1024
                    b0 = kb * 512
                    if b0 >= mstart:
                        mj = xt_global * 1024 + (b0 - mstart)
                        p.op("vector", lambda e, kb=kb, w=w, b0=b0, mj=mj, pm=pm:
                             e.scalar_tensor_tensor(
                                 out=Sb[:, b0:b0 + w], in0=pm[:, 0:w], scalar=1.0,
                                 in1=masks[:, mj:mj + w], op0=AO.mult, op1=AO.add),
                             inc=("cv", 1))
                    else:
                        p.op("vector", lambda e, kb=kb, w=w, b0=b0, pm=pm:
                             e.tensor_copy(Sb[:, b0:b0 + w], pm[:, 0:w]),
                             inc=("cv", 1))
                p.wait("vector", "cv")
                p.op("vector", lambda e, E=E: e.reduce_max(
                    g.stat[:, 0:1], Sb[:, 0:E], axis=AX.X), inc=("cv", 1))
                p.wait("scalar", "cv")
                p.wait("scalar", "ct")
                p.op("scalar", lambda e: e.activation(
                    out=g.stat[:, 1:2], in_=g.stat[:, 0:1], func=AF.Identity,
                    scale=-SC), inc=("cs", 1))
                p.wait("scalar", "cs")
                p.op("scalar", lambda e, E=E: e.activation(
                    out=Pb[:, 0:E], in_=Sb[:, 0:E], func=AF.Exp,
                    scale=SC, bias=g.stat[:, 1:2], accum_out=g.stat[:, 2:3]),
                    inc=("cs", 1))
                p.wait("vector", "cs")
                p.op("vector", lambda e: e.reciprocal(g.stat[:, 3:4], g.stat[:, 2:3]),
                     inc=("cv", 1))
                p.wait("scalar", "cv")
                p.op("scalar", lambda e, E=E: e.activation(
                    out=Sb[:, 0:E], in_=Pb[:, 0:E], func=AF.Identity,
                    scale=g.stat[:, 3:4]), inc=("cs", 1))
                p.wait("tensor", "cs")
                for k in range(E // 128):
                    ps = g.psB[:, 1024:1152] if k % 2 == 0 else \
                        g.psB[:, 1536:1664]
                    if k >= 2:
                        p.wait("tensor", "cv", p.tick("cv") - 1)
                    p.op("tensor", lambda e, k=k, ps=ps: e.transpose(
                        ps, Sb[:, k * 128:(k + 1) * 128], g.ident[:]), inc=("ct", 1))
                    p.wait("vector", "ct")
                    p.op("vector", lambda e, k=k, j=j, ps=ps: e.tensor_copy(
                        PT[:, k * 512 + j * 128:k * 512 + (j + 1) * 128], ps),
                        inc=("cv", 1))
                p.wait("tensor", "cv")
            pm = g.psA[:, 0:512] if s == 0 else g.psA[:, 512:1024]
            for k in range(nyt):
                p.op("tensor", lambda e, k=k, pm=pm: e.matmul(
                    pm, vb[:, k * 128:(k + 1) * 128], PT[:, k * 512:(k + 1) * 512],
                    start=(k == 0), stop=(k == nyt - 1)),
                    inc=("ct", 1) if k == nyt - 1 else None)
            p.wait("vector", "ct")
            p.wait("vector", "sg")
            p.op("vector", lambda e, pm=pm: e.tensor_copy(att_st, pm), inc=("cv", 1))
            p.wait("gpsimd", "cv")
            p.op("gpsimd", lambda e, h=h, s=s: e.dma_start(
                out=g.attn_s[h * 128:(h + 1) * 128, s * 512:(s + 1) * 512],
                in_=att_st), inc=("sg", 16))
        p.wait("tensor", "sg", p.tick("sg"))


def emit_wo(p, g):
    x1q = g.pa(F32)
    for t in range(NQT):
        p.wait("gpsimd", "cv", p.tick("cv"))
        p.wait("gpsimd", "ct", p.tick("ct"))
        p.op("gpsimd", lambda e, t=t: e.indirect_dma_start(
            out=x1q[:, t * 2048:(t + 1) * 2048], out_offset=None,
            in_=g.x1_s[:, :],
            in_offset=IndirectOffsetOnAxis(g.qidx_t[:, t:t + 1], 0)),
            inc=("g1", 16))
    p.fence()
    wob = [g.pb(F32)[:, 0:8192], g.pb(F32)[:, 8192:16384]]

    def load_wo(ob, which):
        p.op(which, lambda e, ob=ob: e.dma_start(
            out=wob[ob % 2].rearrange("q (c f) -> q c f", c=16),
            in_=g.wo_in[:, ob * 512:(ob + 1) * 512]
            .rearrange("(c q) f -> q c f", q=128)),
            inc=("sd" if which == "sync" else "sq", 16))

    load_wo(0, "sync")
    aT = g.tt(0, F32)
    for ob in range(4):
        p.wait("tensor", "sd")
        p.wait("tensor", "sq")
        if ob + 1 < 4:
            q = "scalar" if (ob + 1) % 2 else "sync"
            p.wait(q, "ct", p.tick("ct"))
            load_wo(ob + 1, q)
        for t in range(NQT):
            p.wait("sync", "ct", p.tick("ct"))
            lt = p.op("sync", lambda e, t=t: e.dma_start(
                out=aT[:].rearrange("q (c f) -> q c f", c=16),
                in_=g.attn_s[:, t * 128:(t + 1) * 128]
                .rearrange("(c q) f -> q c f", q=128)), inc=("sa", 16))
            p.wait("tensor", "sa", lt)
            pm = g.psA[:, 0:512] if t % 2 == 0 else g.psA[:, 512:1024]
            p.wait("tensor", "cv", p.tick("cv") - 1)
            for c in range(16):
                p.op("tensor", lambda e, t=t, ob=ob, c=c, pm=pm: e.matmul(
                    pm, aT[:, c * 128:(c + 1) * 128],
                    wob[ob % 2][:, c * 512:(c + 1) * 512],
                    start=(c == 0), stop=(c == 15)),
                    inc=("ct", 1) if c == 15 else None)
            p.wait("vector", "ct")
            p.wait("vector", "sg")
            p.op("vector", lambda e, t=t, ob=ob, pm=pm: e.scalar_tensor_tensor(
                out=g.tt(1, F32)[:, 0:512], in0=pm, scalar=1.0,
                in1=x1q[:, t * 2048 + ob * 512:t * 2048 + (ob + 1) * 512],
                op0=AO.mult, op1=AO.add), inc=("cv", 1))
            p.wait("gpsimd", "cv")
            p.op("gpsimd", lambda e, t=t, ob=ob: e.dma_start(
                out=g.x2_s[t * 128:(t + 1) * 128, ob * 512:(ob + 1) * 512],
                in_=g.tt(1, F32)[:, 0:512]), inc=("sg", 16))
    p.fence()



def build_program(dev=None):
    nc = bass.Bass("TRN2", target_bir_lowering=False, debug=False)

    class NS:
        pass

    g = NS()
    g.dev = dev
    g.x_in = nc.dram_tensor("x", [T, D], F32, kind="ExternalInput").ap()
    g.wg_in = nc.dram_tensor("wg", [FI, D], F32, kind="ExternalInput").ap()
    g.wu_in = nc.dram_tensor("wu", [FI, D], F32, kind="ExternalInput").ap()
    g.wd_in = nc.dram_tensor("wd", [D, FI], F32, kind="ExternalInput").ap()
    g.wq_in = nc.dram_tensor("wqa", [D, D], F32, kind="ExternalInput").ap()
    g.wkvd_in = nc.dram_tensor("wkvd", [D, R], F32, kind="ExternalInput").ap()
    g.kvw_in = nc.dram_tensor("kvw", [1, R], F32, kind="ExternalInput").ap()
    g.wkvu_in = nc.dram_tensor("wkvu", [R, 2 * D], F32, kind="ExternalInput").ap()
    g.wo_in = nc.dram_tensor("wo", [D, D], F32, kind="ExternalInput").ap()
    g.qidx_in = nc.dram_tensor("qidx", [TQ, 1], I32, kind="ExternalInput").ap()
    g.mask_in = nc.dram_tensor("masks", [8 * 128, 1024], F32,
                               kind="ExternalInput").ap()
    g.out = nc.dram_tensor("x3", [TQ, D], F32, kind="ExternalOutput").ap()
    g.dev_out = None
    if dev is not None:
        shp = {"consts": [128, R], "means": [128, 8], "quant": [256, D],
               "quantd": [256, FI],
               "ffnA": [128, 2048], "h": [256, FI], "wblk": [128, 2048],
               "x1": [T, D], "ln1": [T, D], "latT": [128, 2 * T], "q": [TQ, D],
               "attn": [D, TQ], "x2": [TQ, D]}[dev]
        g.dev_out = nc.dram_tensor("dbg", shp, F32, kind="ExternalOutput").ap()

    g.wgq_s = nc.dram_tensor("wgq_s", [FI, D], BF16).ap()
    g.wuq_s = nc.dram_tensor("wuq_s", [FI, D], BF16).ap()
    g.wdq_s = nc.dram_tensor("wdq_s", [D, FI], BF16).ap()
    g.wdqT_s = nc.dram_tensor("wdqT_s", [FI, D], BF16).ap()
    g.h_s = nc.dram_tensor("h_s", [T, FI], F32).ap()
    g.x1_s = nc.dram_tensor("x1_s", [T, D], F32).ap()
    g.ln1_s = nc.dram_tensor("ln1_s", [T, D], F32).ap()
    g.q_s = nc.dram_tensor("q_s", [TQ, D], F32).ap()
    g.attn_s = nc.dram_tensor("attn_s", [D, TQ], F32).ap()
    g.x2_s = nc.dram_tensor("x2_s", [TQ, D], F32).ap()

    from contextlib import ExitStack
    with ExitStack() as stack:
        sems = {}
        for name in ("sd", "sq", "sg", "cv", "cs", "ct", "cg", "st",
                     "sa", "sb", "sc", "se", "ha", "hb", "g1"):
            sems[name] = stack.enter_context(nc.semaphore(name))

        sb = lambda n, shp, dt: stack.enter_context(nc.sbuf_tensor(n, list(shp), dt))
        g.ident = sb("ident", (128, 128), F32)
        g.identb = sb("identb", (128, 128), BF16)
        g.kvwb = sb("kvwb", (128, R), F32)
        g.c127r = sb("c127r", (128, 1), F32)
        g.eps5 = sb("eps5", (128, 1), F32)
        g.eps6 = sb("eps6", (128, 1), F32)
        g.ones = sb("ones", (128, 1), F32)
        g.onesr = sb("onesr", (1, 128), F32)
        g.stat = sb("stat", (128, 16), F32)
        g.msum = sb("msum", (128, 54), F32)
        g.zero1 = sb("zero1", (128, 1), F32)
        g.ws3 = sb("ws3", (128, 8), F32)
        g.xscol = sb("xscol", (128, 48), F32)
        g.hmx = sb("hmx", (128, 256), F32)
        g.qidx_t = sb("qidx_t", (128, 8), I32)
        g.recipd = sb("recipd", (128, 8), F32)

        PA = sb("PA", (128, 16384), F32)
        PB = sb("PB", (128, 16384), F32)
        PC = sb("PC", (128, 8192), F32)
        TT = sb("TT", (128, 8192), F32)
        g._PA, g._PB, g._PC, g._TT = PA, PB, PC, TT
        g.pa = lambda dt: PA[:].bitcast(dt) if dt != F32 else PA[:]
        g.pb = lambda dt: PB[:].bitcast(dt) if dt != F32 else PB[:]
        g.pc = lambda dt: PC[:].bitcast(dt) if dt != F32 else PC[:]
        g.latT = PC[:, 0:2 * T]
        g.tt = lambda i, dt: (TT[:] if i is None else
                              TT[:, i * 2048:(i + 1) * 2048])
        g.tt_q = lambda dt: TT[:, 6144:6272]

        g.psA = stack.enter_context(nc.psum_tensor("psA", [128, 2048], F32))[:]
        g.psB = stack.enter_context(nc.psum_tensor("psB", [128, 2048], F32))[:]

        block = stack.enter_context(nc.Block())

        for name in ("sync", "scalar", "vector", "tensor", "gpsimd"):
            def make(nm):
                def f(e):
                    emit_all(P(nm, e, sems), g)
                return f
            getattr(block, name)(make(name))

    return nc



_CHUNKS = [[0, 3], [1, 2]]


def _host_meta():
    metas = []
    for role in range(2):
        ch = _CHUNKS[role]
        qidx = np.concatenate([np.arange(c * 512, (c + 1) * 512, dtype=np.int32)
                               for c in ch])[:, None]
        masks = np.zeros((8 * 128, 1024), dtype=np.float32)
        for s in range(2):
            for j in range(4):
                E = EXT[s][j]
                qpos = qidx[s * 512 + j * 128: s * 512 + (j + 1) * 128, 0]
                ycols = np.arange(E - 1024, E)
                m = np.where(ycols[None, :] <= qpos[:, None], 0.0, NEG)
                masks[(s * 4 + j) * 128:(s * 4 + j + 1) * 128, :] = m
        metas.append((qidx, masks))
    return metas


_NC_CACHE = {}


def _get_nc(dev=None):
    if dev not in _NC_CACHE:
        _NC_CACHE[dev] = build_program(dev)
    return _NC_CACHE[dev]


def kernel(L0_attn_x, L0_attn_Wq, L0_attn_Wkv_down, L0_attn_kv_norm_w,
           L0_attn_Wkv_up, L0_attn_Wo, L0_ffn_w_gate, L0_ffn_w_up,
           L0_ffn_w_down, _dev=None):
    x = np.ascontiguousarray(np.asarray(L0_attn_x, dtype=np.float32))
    wq = np.ascontiguousarray(np.asarray(L0_attn_Wq, dtype=np.float32))
    wkvd = np.ascontiguousarray(np.asarray(L0_attn_Wkv_down, dtype=np.float32))
    kvw = np.ascontiguousarray(np.asarray(L0_attn_kv_norm_w,
                                          dtype=np.float32)).reshape(1, R)
    wkvu = np.ascontiguousarray(np.asarray(L0_attn_Wkv_up, dtype=np.float32))
    wo = np.ascontiguousarray(np.asarray(L0_attn_Wo, dtype=np.float32))
    wg = np.ascontiguousarray(np.asarray(L0_ffn_w_gate, dtype=np.float32))
    wu = np.ascontiguousarray(np.asarray(L0_ffn_w_up, dtype=np.float32))
    wd = np.ascontiguousarray(np.asarray(L0_ffn_w_down, dtype=np.float32))

    metas = _host_meta()
    nc = _get_nc(_dev)
    in_maps = []
    for core in range(8):
        b, role = core // 2, core % 2
        qidx, masks = metas[role]
        in_maps.append({
            "x": np.ascontiguousarray(x[b]),
            "wg": wg, "wu": wu, "wd": wd, "wqa": wq, "wkvd": wkvd,
            "kvw": kvw, "wkvu": wkvu, "wo": wo,
            "qidx": qidx, "masks": masks,
        })
    res = run_bass_kernel_spmd(nc, in_maps, core_ids=list(range(8)))
    if _dev is not None:
        return res
    out = np.empty((4, T, D), dtype=np.float32)
    for core in range(8):
        b, role = core // 2, core % 2
        x3 = res.results[core]["x3"]
        for si, c in enumerate(_CHUNKS[role]):
            out[b, c * 512:(c + 1) * 512, :] = x3[si * 512:(si + 1) * 512, :]
    return out


# version 05
# speedup vs baseline: 1.0293x; 1.0293x over previous
import numpy as np
import ml_dtypes
import concourse.bass as bass
import concourse.mybir as mybir
from concourse.bass import IndirectOffsetOnAxis
from concourse.bass_utils import run_bass_kernel_spmd

F32 = mybir.dt.float32
BF16 = mybir.dt.bfloat16
I32 = mybir.dt.int32
AF = mybir.ActivationFunctionType
AO = mybir.AluOpType
AX = mybir.AxisListType

D = 2048
T = 2048
TQ = 1024
FI = 8192
R = 256
H = 16
NT = T // 128
NQT = TQ // 128
SC = float(1.0 / np.sqrt(np.float64(128.0)))
NEG = -1.0e30

EXT = [[640 + 128 * j for j in range(4)], [1664 + 128 * j for j in range(4)]]


class P:

    def __init__(self, cur, handle, sems):
        self.cur = cur
        self.h = handle
        self.sems = sems
        self.cnt = {k: 0 for k in sems}

    def op(self, eng, fn, inc=None):
        ins = fn(self.h) if self.cur == eng else None
        if inc is not None:
            s, a = inc
            self.cnt[s] += a
            if ins is not None:
                ins.then_inc(self.sems[s], a)
            return self.cnt[s]
        return None

    def tick(self, sem):
        return self.cnt[sem]

    def wait(self, eng, sem, val=None):
        if val is None:
            val = self.cnt[sem]
        if self.cur == eng and val > 0:
            self.h.wait_ge(self.sems[sem], val)

    def fence(self, engs=("sync", "scalar", "vector", "tensor", "gpsimd")):
        for e in engs:
            for s in self.sems:
                self.wait(e, s)


def ceil_div(a, b):
    return (a + b - 1) // b



def emit_all(p, g):
    emit_consts(p, g)
    if g.dev == "consts":
        p.fence()
        p.op("gpsimd", lambda e: e.dma_start(out=g.dev_out, in_=g.kvwb[:]),
             inc=("st", 16))
        p.wait("gpsimd", "st")
        return
    emit_weight_means(p, g)
    if g.dev == "means":
        p.fence()
        p.op("gpsimd", lambda e: e.dma_start(out=g.dev_out, in_=g.ws3[:]),
             inc=("st", 16))
        p.wait("gpsimd", "st")
        return
    emit_quantize(p, g)
    p.fence()
    if g.dev == "quant":
        for r0 in (0, 4096):
            p.wait("gpsimd", "st")
            p.op("gpsimd", lambda e, r0=r0: e.dma_start(
                out=g.tt(0, F32)[:].bitcast(BF16)[:, 0:2048],
                in_=g.wgq_s[r0:r0 + 128, :]), inc=("sg", 16))
            p.wait("vector", "sg")
            p.op("vector", lambda e: e.tensor_copy(g.tt(1, F32)[:],
                 g.tt(0, F32)[:].bitcast(BF16)[:, 0:2048]), inc=("cv", 1))
            p.wait("gpsimd", "cv")
            p.op("gpsimd", lambda e, r0=r0: e.dma_start(
                out=g.dev_out[(r0 // 4096) * 128:(r0 // 4096) * 128 + 128, :],
                in_=g.tt(1, F32)[:]), inc=("st", 16))
        p.wait("gpsimd", "st")
        return
    if g.dev == "quantd":
        for i, r0 in enumerate((0, 1024)):
            p.wait("gpsimd", "st")
            p.op("gpsimd", lambda e, r0=r0: e.dma_start(
                out=g.pa(BF16)[:, 0:8192],
                in_=g.wdq_s[r0:r0 + 128, :]), inc=("sg", 16))
            p.wait("vector", "sg")
            p.op("vector", lambda e: e.tensor_copy(g.tt(None, F32)[:],
                 g.pa(BF16)[:, 0:8192]), inc=("cv", 1))
            p.wait("gpsimd", "cv")
            p.op("gpsimd", lambda e, i=i: e.dma_start(
                out=g.dev_out[i * 128:(i + 1) * 128, :],
                in_=g.tt(None, F32)[:]), inc=("st", 16))
        p.wait("gpsimd", "st")
        return
    emit_ffn(p, g, app=1)
    p.fence()
    if g.dev in ("ffnA", "h", "wblk"):
        p.wait("gpsimd", "st")
        return
    if g.dev == "x1":
        emit_dev_copy(p, g, g.x1_s, T)
        return
    emit_latent(p, g)
    p.fence()
    if g.dev == "ln1":
        emit_dev_copy(p, g, g.ln1_s, T)
        return
    if g.dev == "latT":
        emit_dev_latT(p, g)
        return
    emit_qproj(p, g)
    p.fence()
    if g.dev == "q":
        emit_dev_copy(p, g, g.q_s, TQ)
        return
    emit_attention(p, g)
    p.fence()
    if g.dev == "attn":
        emit_dev_copy(p, g, g.attn_s, D)
        return
    emit_wo(p, g)
    p.fence()
    if g.dev == "x2":
        emit_dev_copy(p, g, g.x2_s, TQ)
        return
    emit_ffn(p, g, app=2)
    p.fence()
    p.wait("gpsimd", "st")


def emit_dev_copy(p, g, src_dram, rows):
    p.fence()
    ncol = src_dram.shape[1]
    for r0 in range(0, rows, 128):
        p.wait("gpsimd", "sg")
        p.wait("gpsimd", "st")
        p.op("gpsimd", lambda e, r0=r0: e.dma_start(
            out=g.tt(None, F32)[:, 0:ncol], in_=src_dram[r0:r0 + 128, :]),
            inc=("sg", 16))
        p.wait("gpsimd", "sg")
        p.op("gpsimd", lambda e, r0=r0: e.dma_start(
            out=g.dev_out[r0:r0 + 128, :], in_=g.tt(None, F32)[:, 0:ncol]),
            inc=("st", 16))
    p.wait("gpsimd", "st")


def emit_dev_latT(p, g):
    p.fence()
    p.op("gpsimd", lambda e: e.dma_start(out=g.dev_out, in_=g.latT[:]),
         inc=("st", 16))
    p.wait("gpsimd", "st")


def emit_consts(p, g):
    p.op("gpsimd", lambda e: e.memset(g.ident[:], 0.0), inc=("cg", 1))
    p.wait("gpsimd", "cg")
    p.op("gpsimd", lambda e: e.affine_select(
        out=g.ident[:], in_=g.ident[:], compare_op=AO.not_equal,
        fill=1.0, base=0, pattern=[[-1, 128]], channel_multiplier=1), inc=("cg", 1))
    p.op("gpsimd", lambda e: e.memset(g.identb[:], 0.0), inc=("cg", 1))
    p.wait("gpsimd", "cg")
    p.op("gpsimd", lambda e: e.affine_select(
        out=g.identb[:], in_=g.identb[:], compare_op=AO.not_equal,
        fill=1.0, base=0, pattern=[[-1, 128]], channel_multiplier=1), inc=("cg", 1))
    p.op("sync", lambda e: e.dma_start(out=g.kvwb[0:1, :], in_=g.kvw_in),
         inc=("sd", 16))
    p.op("sync", lambda e: e.dma_start(out=g.qidx_t[:],
         in_=g.qidx_in.rearrange("(a p) o -> p (a o)", p=128),
         allow_slow_non_contiguous=True), inc=("sd", 16))
    p.op("vector", lambda e: e.memset(g.onesr[:], 1.0), inc=("cv", 1))
    p.wait("tensor", "sd")
    p.wait("tensor", "cv")
    p.op("tensor", lambda e: e.matmul(g.psA[:, 0:256], g.onesr[:], g.kvwb[0:1, :],
         start=True, stop=True), inc=("ct", 1))
    p.wait("vector", "ct")
    p.op("vector", lambda e: e.tensor_copy(g.kvwb[:], g.psA[:, 0:256]),
         inc=("cv", 1))
    p.op("vector", lambda e: e.memset(g.eps5[:], 1.0e-5))
    p.op("vector", lambda e: e.memset(g.eps6[:], 1.0e-6))
    p.op("vector", lambda e: e.memset(g.zero1[:], 0.0))
    p.op("vector", lambda e: e.memset(g.ws3[:, 7:8], 0.0))
    p.op("vector", lambda e: e.memset(g.ones[:], 1.0))
    p.op("vector", lambda e: e.memset(g.stat[:, 0:1], 127.0), inc=("cv", 1))
    p.wait("vector", "cv")
    p.op("vector", lambda e: e.reciprocal(g.c127r[:], g.stat[:, 0:1]),
         inc=("cv", 1))
    p.wait("vector", "cv")


def emit_weight_means(p, g):
    CH = 4096
    NCH = 131072 // CH
    cb_t = [0, 0]
    buf_t = [0, 0]
    for wi, w_ap in enumerate((g.wg_in, g.wu_in, g.wd_in)):
        flat = w_ap.rearrange("(q a) f -> q (a f)", q=128)
        for k in range(NCH):
            half = k % 2
            buf = g.pa(F32)[:, half * CH:(half + 1) * CH]
            cb = g.pb(F32)[:, 0:CH + 1] if half == 0 else \
                g.pb(F32)[:, CH + 1:2 * CH + 2]
            sem = "sa" if half == 0 else "sb"
            slot = k // (NCH // 2)
            acol = g.msum[:, wi * 2 + slot:wi * 2 + slot + 1]
            if buf_t[half]:
                p.wait("sync", "cs", buf_t[half])
            lt = p.op("sync", lambda e, b=buf, sp=flat[:, k * CH:(k + 1) * CH]:
                      e.dma_start(out=b, in_=sp), inc=(sem, 16))
            p.wait("scalar", sem, lt)
            if cb_t[half]:
                p.wait("scalar", "cv", cb_t[half])
            buf_t[half] = p.op("scalar", lambda e, b=buf, cb=cb: e.activation(
                out=cb[:, 1:CH + 1], in_=b, func=AF.Abs), inc=("cs", 1))
            p.wait("vector", "cs")
            p.wait("vector", "cv")
            if k % (NCH // 2) == 0:
                p.op("vector", lambda e, cb=cb: e.memset(cb[:, 0:1], 0.0),
                     inc=("cv", 1))
            else:
                p.op("vector", lambda e, cb=cb, a=acol: e.tensor_copy(
                    cb[:, 0:1], a), inc=("cv", 1))
            p.wait("vector", "cv")
            cb_t[half] = p.op("vector", lambda e, cb=cb, a=acol: e.reduce_sum(
                a, cb[:, 0:CH + 1], axis=AX.X), inc=("cv", 1))
    p.wait("tensor", "cv")
    p.wait("tensor", "cg")
    for wi in range(3):
        for j in range(2):
            ps = g.psA[0:1, wi * 256 + j * 128:wi * 256 + (j + 1) * 128]
            p.op("tensor", lambda e, wi=wi, j=j, ps=ps: e.transpose(
                ps, g.msum[:, wi * 2 + j:wi * 2 + j + 1], g.ident[:]),
                inc=("ct", 1))
    p.wait("vector", "ct")
    row = g.tt(None, F32)[0:1, 0:256]
    ga = g.tt(None, F32)[0:1, 256:320]
    gb = g.tt(None, F32)[0:1, 320:384]
    _r1 = lambda ap: ap.rearrange("p (a t) -> p a t", t=1)
    for wi in range(3):
        rv = row.rearrange("p (a t) -> p a t", t=2)
        for j in range(2):
            ps = g.psA[0:1, wi * 256 + j * 128:wi * 256 + (j + 1) * 128]
            p.op("vector", lambda e, ps=ps, rv=rv, j=j: e.tensor_copy(
                rv[:, :, j:j + 1], _r1(ps)), inc=("cv", 1))
        p.wait("vector", "cv")
        r4 = row.rearrange("p (a t) -> p a t", t=4)
        p.op("vector", lambda e, r4=r4: e.tensor_tensor(
            out=_r1(ga), in0=r4[:, :, 0:1], in1=r4[:, :, 1:2], op=AO.add),
            inc=("cv", 1))
        p.wait("vector", "cv")
        p.op("vector", lambda e, r4=r4: e.tensor_tensor(
            out=_r1(gb), in0=_r1(ga), in1=r4[:, :, 2:3], op=AO.add),
            inc=("cv", 1))
        p.wait("vector", "cv")
        p.op("vector", lambda e, r4=r4: e.tensor_tensor(
            out=_r1(ga), in0=_r1(gb), in1=r4[:, :, 3:4], op=AO.add),
            inc=("cv", 1))
        p.wait("vector", "cv")
        p.op("vector", lambda e, wi=wi: e.reduce_sum(
            g.stat[0:1, 1 + wi:2 + wi], ga, axis=AX.X), inc=("cv", 1))
        p.wait("vector", "cv")
    p.op("vector", lambda e: e.tensor_scalar(
        out=g.stat[0:1, 8:11], in0=g.stat[0:1, 1:4], scalar1=float(2.0 ** -24),
        scalar2=1.0e-5, op0=AO.mult, op1=AO.add), inc=("cv", 1))
    p.wait("vector", "cv")
    p.op("vector", lambda e: e.reciprocal(g.stat[0:1, 11:14], g.stat[0:1, 8:11]),
         inc=("cv", 1))
    p.wait("tensor", "cv")
    p.op("tensor", lambda e: e.matmul(g.psA[:, 256:259], g.onesr[:],
         g.stat[0:1, 11:14], start=True, stop=True))
    p.op("tensor", lambda e: e.matmul(g.psA[:, 259:262], g.onesr[:],
         g.stat[0:1, 8:11], start=True, stop=True), inc=("ct", 1))
    p.wait("vector", "ct")
    p.op("vector", lambda e: e.tensor_copy(g.ws3[:, 0:3], g.psA[:, 256:259]))
    p.op("vector", lambda e: e.tensor_copy(g.ws3[:, 3:6], g.psA[:, 259:262]),
         inc=("cv", 1))
    p.wait("vector", "cv")
    p.op("vector", lambda e: e.tensor_scalar(
        out=g.ws3[:, 6:7], in0=g.ws3[:, 5:6], scalar1=0.5, scalar2=None,
        op0=AO.mult))
    p.op("vector", lambda e: e.tensor_scalar(
        out=g.ws3[:, 7:8], in0=g.ws3[:, 5:6], scalar1=-0.5, scalar2=None,
        op0=AO.mult), inc=("cv", 1))
    p.wait("vector", "cv")
    p.wait("sync", "cs")
    p.wait("sync", "cv")
    p.wait("scalar", "cv")
    p.wait("tensor", "cv")
    p.wait("scalar", "cv")
    p.wait("sync", "cv")
    p.wait("tensor", "cv")


def emit_quantize(p, g):
    jobs = []
    for w_ap, dst, nrow, ncol, wi in ((g.wg_in, g.wgq_s, FI, D, 0),
                                      (g.wu_in, g.wuq_s, FI, D, 1),
                                      (g.wd_in, g.wdq_s, D, FI, 2)):
        rps = 128 * (8192 // ncol)
        for s in range(nrow // rps):
            jobs.append((w_ap, dst, rps, s, wi))
    wqb = g.pc(BF16)[:, 0:8192]
    ri = g.pb(I32)[:, 8192:16384]
    junk = g.pb(F32)[:, 0:8192]
    zf2 = g.pb(F32)[:, 8192:16384]
    store_tick = [0]
    freed = []
    for k, (w_ap, dst, rps, s, wi) in enumerate(jobs):
        buf = g.pa(F32)[:, 0:8192] if k % 2 == 0 else g.pa(F32)[:, 8192:16384]
        sem = "sa" if k % 2 == 0 else "sb"
        if k >= 2:
            fs, ft = freed[k - 2]
            p.wait("sync", fs, ft)
        src = w_ap[s * rps:(s + 1) * rps, :].rearrange("(q a) f -> q (a f)", q=128)
        lt = p.op("sync", lambda e, b=buf, sp=src: e.dma_start(out=b, in_=sp),
                  inc=(sem, 16))
        p.wait("scalar", sem, lt)
        p.wait("scalar", "cv")
        if wi < 2:
            p.op("scalar", lambda e, b=buf, wi=wi: e.activation(
                out=ri, in_=b, func=AF.Identity, scale=g.ws3[:, wi:wi + 1]),
                inc=("cs", 1))
            freed.append(("cs", p.tick("cs")))
            p.wait("vector", "cs")
            p.wait("vector", "sg", store_tick[0])
            p.op("vector", lambda e: e.tensor_scalar(
                out=wqb, in0=ri, scalar1=1.0, scalar2=-1.0, op0=AO.min,
                op1=AO.max), inc=("cv", 1))
        else:
            p.op("scalar", lambda e, b=buf: e.activation(
                out=junk, in_=b, func=AF.Abs), inc=("cs", 1))
            p.wait("scalar", "cs")
            p.op("scalar", lambda e: e.activation(
                out=zf2, in_=junk, func=AF.Relu, bias=g.ws3[:, 7:8]),
                inc=("cs", 1))
            p.wait("vector", "cs")
            p.op("vector", lambda e, b=buf: e.scalar_tensor_tensor(
                out=junk, in0=zf2, scalar=1.0e30, in1=b, op0=AO.mult,
                op1=AO.mult), inc=("cv", 1))
            freed.append(("cv", p.tick("cv")))
            p.wait("vector", "cv")
            p.wait("vector", "sg", store_tick[0])
            p.op("vector", lambda e: e.tensor_scalar(
                out=wqb, in0=junk, scalar1=1.0, scalar2=-1.0, op0=AO.min,
                op1=AO.max), inc=("cv", 1))
        p.wait("gpsimd", "cv")
        dstap = dst[s * rps:(s + 1) * rps, :].rearrange("(q a) f -> q (a f)", q=128)
        store_tick[0] = p.op("gpsimd", lambda e, dp=dstap: e.dma_start(
            out=dp, in_=wqb), inc=("sg", 16))
    p.wait("sync", "sg")
    p.wait("sync", "cs")
    p.wait("sync", "cv")
    tb0 = g.pa(BF16)[:, 0:2048]
    tb1 = g.pa(BF16)[:, 2048:4096]
    st_tick = [0, 0]
    for c in range(FI // 128):
        tb = tb0 if c % 2 == 0 else tb1
        rsem = "sa" if c % 2 == 0 else "sb"
        ssem = "sc" if c % 2 == 0 else "se"
        if c >= 2:
            p.wait("sync", ssem, st_tick[c % 2])
        rt = p.op("sync", lambda e, c=c, tb=tb: e.dma_start(
            out=tb, in_=g.wdq_s[:, c * 128:(c + 1) * 128], transpose=True),
            inc=(rsem, 16))
        p.wait("scalar", rsem, rt)
        st_tick[c % 2] = p.op("scalar", lambda e, c=c, tb=tb: e.dma_start(
            out=g.wdqT_s[c * 128:(c + 1) * 128, :], in_=tb), inc=(ssem, 16))
    p.wait("scalar", "sc")
    p.wait("scalar", "se")


def emit_ffn(p, g, app):
    Tt = T if app == 1 else TQ
    nt = Tt // 128
    src = g.x_in if app == 1 else g.x2_s
    dst = g.x1_s if app == 1 else g.out

    xqT = g.pa(BF16)
    for t in range(nt):
        emit_ln_quant_tile(p, g, src, t, nt, xqT)

    p.fence()
    if g.dev == "ffnA" and app == 1:
        p.op("vector", lambda e: e.tensor_copy(g.tt(0, F32)[:],
             xqT[:, 0:2048]), inc=("cv", 1))
        p.wait("gpsimd", "cv")
        p.op("gpsimd", lambda e: e.dma_start(out=g.dev_out, in_=g.tt(0, F32)[:]),
             inc=("st", 16))
        p.wait("gpsimd", "st")
        return

    emit_gate_up(p, g, app, nt, xqT)

    p.fence()
    if g.dev == "h" and app == 1:
        for i, r0 in enumerate((0, 1024)):
            p.wait("gpsimd", "st")
            p.wait("gpsimd", "cv", p.tick("cv"))
            p.op("gpsimd", lambda e, r0=r0: e.dma_start(
                out=g.tt(None, F32)[:, 0:8192], in_=g.h_s[r0:r0 + 128, :]),
                inc=("sg", 16))
            p.wait("gpsimd", "sg")
            p.op("gpsimd", lambda e, i=i: e.dma_start(
                out=g.dev_out[i * 128:(i + 1) * 128, :],
                in_=g.tt(None, F32)[:, 0:8192]), inc=("st", 16))
        p.wait("gpsimd", "st")
        return
    for m0 in range(0, nt, 4):
        emit_down_macro(p, g, app, nt, m0, src, dst)
    p.fence()


def emit_ln_quant_tile(p, g, src, t, nt, xqT):
    xt = g.tt(0, F32)
    lnt = g.tt(1, F32)
    sq = g.tt(2, F32)
    ri = g.pb(I32)[:, 0:2048]
    xqb = g.pb(BF16)[:, 8192:10240]
    st = g.stat
    p.wait("sync", "cv", p.tick("cv"))
    p.wait("sync", "cs", p.tick("cs"))
    p.op("sync", lambda e: e.dma_start(out=xt[:], in_=src[t * 128:(t + 1) * 128, :]),
         inc=("sd", 16))
    p.wait("vector", "sd")
    p.op("vector", lambda e: e.reduce_sum(st[:, 0:1], xt[:], axis=AX.X),
         inc=("cv", 1))
    p.wait("scalar", "sd")
    p.op("scalar", lambda e: e.activation(out=sq[:], in_=xt[:], func=AF.Square,
         accum_out=st[:, 1:2]), inc=("cs", 1))
    p.wait("vector", "cv")
    p.wait("vector", "cs")
    p.op("vector", lambda e: e.tensor_scalar(
        out=st[:, 2:3], in0=st[:, 0:1], scalar1=float(1.0 / D), scalar2=None,
        op0=AO.mult))
    p.op("vector", lambda e: e.tensor_scalar(
        out=st[:, 3:4], in0=st[:, 1:2], scalar1=float(1.0 / D), scalar2=None,
        op0=AO.mult), inc=("cv", 1))
    p.wait("vector", "cv")
    p.op("vector", lambda e: e.tensor_tensor(
        out=st[:, 4:5], in0=st[:, 2:3], in1=st[:, 2:3], op=AO.mult),
        inc=("cv", 1))
    p.wait("vector", "cv")
    p.op("vector", lambda e: e.tensor_sub(st[:, 5:6], st[:, 3:4], st[:, 4:5]),
         inc=("cv", 1))
    p.wait("scalar", "cv")
    p.op("scalar", lambda e: e.activation(out=st[:, 6:7], in_=st[:, 5:6],
         func=AF.Sqrt, bias=g.eps5[:]), inc=("cs", 1))
    p.wait("vector", "cs")
    p.op("vector", lambda e: e.reciprocal(st[:, 7:8], st[:, 6:7]), inc=("cv", 1))
    p.wait("vector", "cv")
    p.op("vector", lambda e: e.scalar_tensor_tensor(
        out=st[:, 8:9], in0=st[:, 2:3], scalar=-1.0, in1=st[:, 7:8],
        op0=AO.mult, op1=AO.mult), inc=("cv", 1))
    p.wait("scalar", "cv")
    p.op("scalar", lambda e: e.activation(out=lnt[:], in_=xt[:], func=AF.Identity,
         scale=st[:, 7:8], bias=st[:, 8:9]), inc=("cs", 1))
    p.wait("vector", "cs")
    p.op("vector", lambda e: e.reduce_max(st[:, 9:10], lnt[:], axis=AX.X,
         apply_absolute_value=True), inc=("cv", 1))
    p.wait("vector", "cv")
    p.op("vector", lambda e: e.tensor_scalar(
        out=st[:, 10:11], in0=st[:, 9:10], scalar1=g.c127r[:], scalar2=g.eps5[:],
        op0=AO.mult, op1=AO.add), inc=("cv", 1))
    p.wait("vector", "cv")
    p.op("vector", lambda e: e.reciprocal(st[:, 11:12], st[:, 10:11]),
         inc=("cv", 1))
    p.wait("vector", "cv")
    p.op("vector", lambda e: e.tensor_tensor(
        out=g.xscol[:, t:t + 1], in0=st[:, 10:11], in1=g.ws3[:, 3:4], op=AO.mult))
    p.op("vector", lambda e: e.tensor_tensor(
        out=g.xscol[:, 16 + t:17 + t], in0=st[:, 10:11], in1=g.ws3[:, 4:5],
        op=AO.mult), inc=("cv", 1))
    p.wait("scalar", "cv")
    p.op("scalar", lambda e: e.activation(out=sq[:], in_=lnt[:], func=AF.Identity,
         scale=st[:, 11:12]), inc=("cs", 1))
    p.wait("vector", "cs")
    p.op("vector", lambda e: e.tensor_copy(ri, sq[:]), inc=("cv", 1))
    p.wait("vector", "cv")
    p.op("vector", lambda e: e.tensor_copy(xqb, ri), inc=("cv", 1))
    p.wait("tensor", "cv")
    for c in range(16):
        ps = (g.psB[:, 0:64] if c % 2 == 0 else g.psB[:, 512:576]).bitcast(BF16)
        if c >= 2:
            p.wait("tensor", "cv", p.tick("cv") - 1)
        p.op("tensor", lambda e, c=c, ps=ps: e.transpose(
            ps, xqb[:, c * 128:(c + 1) * 128], g.identb[:]),
            inc=("ct", 1))
        p.wait("vector", "ct")
        p.op("vector", lambda e, c=c, ps=ps: e.tensor_copy(
            xqT[:, c * (nt * 128) + t * 128: c * (nt * 128) + (t + 1) * 128],
            ps), inc=("cv", 1))
    p.wait("tensor", "cv")


def emit_gate_up(p, g, app, nt, xqT):
    wbuf = [g.pb(BF16)[:, 0:16384], g.pb(BF16)[:, 16384:32768]]
    silu_t = g.tt(0, F32)[:, 0:512]
    hbuf = [g.tt(1, F32)[:, 0:512], g.tt(1, F32)[:, 512:1024]]
    nob = 16

    def load_block(ob, which):
        b = wbuf[ob % 2]
        sem = "sa" if ob % 2 == 0 else "sb"
        lt = None
        for c in range(16):
            for mi, scr in enumerate((g.wgq_s, g.wuq_s)):
                lt = p.op(which, lambda e, b=b, c=c, mi=mi, scr=scr, ob=ob:
                          e.dma_start(
                    out=b[:, mi * 8192 + c * 512:mi * 8192 + (c + 1) * 512],
                    in_=scr[ob * 512:(ob + 1) * 512, c * 128:(c + 1) * 128],
                    transpose=True),
                    inc=(sem, 16))
        return sem, lt

    blk_tick = [None, None]
    blk_tick[0] = load_block(0, "sync")
    hdma_tick = [0, 0]
    if g.dev == "wblk" and app == 1:
        blk_tick[1] = load_block(1, "scalar")
        p.wait("vector", *blk_tick[0])
        p.wait("vector", *blk_tick[1])
        p.op("vector", lambda e: e.tensor_copy(g.tt(0, F32)[:, 0:512],
             wbuf[0][:, 0:512]))
        p.op("vector", lambda e: e.tensor_copy(g.tt(0, F32)[:, 512:1024],
             wbuf[0][:, 8192:8704]))
        p.op("vector", lambda e: e.tensor_copy(g.tt(0, F32)[:, 1024:1536],
             wbuf[1][:, 0:512]))
        p.op("vector", lambda e: e.tensor_copy(g.tt(0, F32)[:, 1536:2048],
             wbuf[1][:, 8192:8704]), inc=("cv", 1))
        p.wait("gpsimd", "cv")
        p.op("gpsimd", lambda e: e.dma_start(out=g.dev_out, in_=g.tt(0, F32)[:]),
             inc=("st", 16))
        p.wait("gpsimd", "st")
        return
    for ob in range(nob):
        b = wbuf[ob % 2]
        p.wait("tensor", *blk_tick[ob % 2])
        for t in range(nt):
            pg = g.psA[:, 0:512] if t % 2 == 0 else g.psA[:, 1024:1536]
            pu = g.psA[:, 512:1024] if t % 2 == 0 else g.psA[:, 1536:2048]
            p.wait("tensor", "cv", p.tick("cv") - 1)
            p.wait("tensor", "cs", p.tick("cs") - 1)
            tg = tu = None
            for c in range(16):
                tg = p.op("tensor", lambda e, b=b, c=c, t=t, pg=pg: e.matmul(
                    pg, xqT[:, c * (nt * 128) + t * 128:c * (nt * 128) + (t + 1) * 128],
                    b[:, c * 512:(c + 1) * 512],
                    start=(c == 0), stop=(c == 15)),
                    inc=("ct", 1) if c == 15 else None)
            for c in range(16):
                tu = p.op("tensor", lambda e, b=b, c=c, t=t, pu=pu: e.matmul(
                    pu, xqT[:, c * (nt * 128) + t * 128:c * (nt * 128) + (t + 1) * 128],
                    b[:, 8192 + c * 512:8192 + (c + 1) * 512],
                    start=(c == 0), stop=(c == 15)),
                    inc=("ct", 1) if c == 15 else None)
            p.wait("scalar", "ct", tg)
            p.wait("scalar", "cv")
            p.op("scalar", lambda e, t=t, pg=pg: e.activation(
                out=silu_t, in_=pg, func=AF.Silu, scale=g.xscol[:, t:t + 1]),
                inc=("cs", 1))
            p.wait("vector", "ct", tu)
            p.wait("vector", "cs")
            hb = hbuf[t % 2]
            hsem = "ha" if t % 2 == 0 else "hb"
            p.wait("vector", hsem, hdma_tick[t % 2])
            p.op("vector", lambda e, t=t, pu=pu, hb=hb: e.scalar_tensor_tensor(
                out=hb, in0=pu, scalar=g.xscol[:, 16 + t:17 + t], in1=silu_t,
                op0=AO.mult, op1=AO.mult), inc=("cv", 1))
            p.wait("vector", "cv")
            p.op("vector", lambda e, t=t, ob=ob, hb=hb: e.reduce_max(
                g.hmx[:, t * 16 + ob:t * 16 + ob + 1], hb, axis=AX.X,
                apply_absolute_value=True), inc=("cv", 1))
            p.wait("gpsimd", "cv")
            hdma_tick[t % 2] = p.op("gpsimd", lambda e, t=t, ob=ob, hb=hb:
                e.dma_start(
                    out=g.h_s[t * 128:(t + 1) * 128, ob * 512:(ob + 1) * 512],
                    in_=hb), inc=(hsem, 16))
            if t == 0 and ob + 1 < nob:
                q = "sync" if (ob + 1) % 2 == 0 else "scalar"
                p.wait(q, "ct", p.tick("ct"))
                blk_tick[(ob + 1) % 2] = load_block(ob + 1, q)
    p.fence()


def emit_down_macro(p, g, app, nt, m0, src, dst):
    ntm = min(4, nt - m0)
    hq_T = g.pa(BF16)
    yacc = g.pc(F32)[:, 0:8192]
    hbuf = g.tt(None, F32)
    hsc = g.pb(F32)[:, 0:8192]
    ri = g.pb(I32)[:, 8192:16384]
    hqb = g.pb(BF16)[:, 0:8192]
    st = g.stat

    for ti in range(ntm):
        t = m0 + ti
        p.wait("sync", "cv", p.tick("cv"))
        p.wait("sync", "cs", p.tick("cs"))
        p.wait("sync", "ct", p.tick("ct"))
        lt = p.op("sync", lambda e, t=t: e.dma_start(
            out=hbuf[:], in_=g.h_s[t * 128:(t + 1) * 128, :]), inc=("sd", 16))
        p.wait("vector", "sd", lt)
        p.wait("scalar", "sd", lt)
        p.op("vector", lambda e, t=t: e.reduce_max(
            st[:, 0:1], g.hmx[:, t * 16:t * 16 + 16], axis=AX.X), inc=("cv", 1))
        p.wait("vector", "cv")
        p.op("vector", lambda e: e.tensor_scalar(
            out=st[:, 1:2], in0=st[:, 0:1], scalar1=g.c127r[:], scalar2=g.eps5[:],
            op0=AO.mult, op1=AO.add), inc=("cv", 1))
        p.wait("vector", "cv")
        p.op("vector", lambda e: e.reciprocal(st[:, 2:3], st[:, 1:2]), inc=("cv", 1))
        p.wait("vector", "cv")
        p.op("vector", lambda e, t=t: e.scalar_tensor_tensor(
            out=g.xscol[:, 32 + (t % 16):33 + (t % 16)], in0=st[:, 1:2], scalar=0.5,
            in1=g.ws3[:, 5:6], op0=AO.mult, op1=AO.mult), inc=("cv", 1))
        p.wait("scalar", "cv")
        p.op("scalar", lambda e: e.activation(out=hsc, in_=hbuf[:],
             func=AF.Identity, scale=st[:, 2:3]), inc=("cs", 1))
        p.wait("vector", "cs")
        p.op("vector", lambda e: e.tensor_copy(ri, hsc), inc=("cv", 1))
        p.wait("vector", "cv")
        p.op("vector", lambda e: e.tensor_copy(hqb, ri), inc=("cv", 1))
        p.wait("tensor", "cv")
        for c in range(64):
            ps = (g.psB[:, 0:64] if c % 2 == 0 else g.psB[:, 512:576]).bitcast(BF16)
            if c >= 2:
                p.wait("tensor", "cv", p.tick("cv") - 1)
            p.op("tensor", lambda e, c=c, ps=ps: e.transpose(
                ps, hqb[:, c * 128:(c + 1) * 128],
                g.identb[:]), inc=("ct", 1))
            p.wait("vector", "ct")
            p.op("vector", lambda e, c=c, ti=ti, ps=ps: e.tensor_copy(
                hq_T[:, ti * 8192 + c * 128:ti * 8192 + (c + 1) * 128],
                ps), inc=("cv", 1))
    p.fence()

    wdb = [g.pb(BF16)[:, 0:16384], g.pb(BF16)[:, 16384:32768]]

    def load_half(hb, which):
        b = wdb[hb % 2]
        sem = "sa" if hb % 2 == 0 else "sb"
        lt = p.op(which, lambda e, b=b, hb=hb: e.dma_start(
            out=b.rearrange("q (c f) -> q c f", c=8),
            in_=g.wdqT_s[hb * 1024:(hb + 1) * 1024, :]
            .rearrange("(c q) f -> q c f", q=128)),
            inc=(sem, 16))
        return sem, lt

    half_tick = [None, None]
    half_tick[0] = load_half(0, "sync")
    for hb in range(8):
        b = wdb[hb % 2]
        p.wait("tensor", *half_tick[hb % 2])
        if hb + 1 < 8:
            q = "sync" if (hb + 1) % 2 == 0 else "scalar"
            p.wait(q, "ct", p.tick("ct"))
            half_tick[(hb + 1) % 2] = load_half(hb + 1, q)
        for ti in range(ntm):
            pm = g.psA if ti % 2 == 0 else g.psB
            p.wait("tensor", "cv", p.tick("cv") - 1)
            for i in range(8):
                ic = hb * 8 + i
                for hgi in range(4):
                    p.op("tensor", lambda e, b=b, i=i, ic=ic, ti=ti, hgi=hgi,
                         pm=pm: e.matmul(
                             pm[:, hgi * 512:(hgi + 1) * 512],
                             hq_T[:, ti * 8192 + ic * 128:ti * 8192 + (ic + 1) * 128],
                             b[:, i * 2048 + hgi * 512:i * 2048 + (hgi + 1) * 512],
                             start=(i == 0), stop=(i == 7)),
                         inc=("ct", 1) if (i == 7 and hgi == 3) else None)
            p.wait("vector", "ct")
            if hb == 0:
                p.op("vector", lambda e, ti=ti, pm=pm: e.tensor_copy(
                    yacc[:, ti * 2048:(ti + 1) * 2048], pm[:, 0:2048]),
                    inc=("cv", 1))
            else:
                p.op("vector", lambda e, ti=ti, pm=pm: e.scalar_tensor_tensor(
                    out=yacc[:, ti * 2048:(ti + 1) * 2048], in0=pm[:, 0:2048],
                    scalar=1.0, in1=yacc[:, ti * 2048:(ti + 1) * 2048],
                    op0=AO.mult, op1=AO.add), inc=("cv", 1))
    p.fence()

    osem = "st" if dst is g.out else "sg"
    for ti in range(ntm):
        t = m0 + ti
        xt = g.tt(0, F32)
        p.wait("sync", "cv", p.tick("cv"))
        lt = p.op("sync", lambda e, t=t: e.dma_start(
            out=xt[:], in_=src[t * 128:(t + 1) * 128, :]), inc=("sd", 16))
        p.wait("vector", "sd", lt)
        p.wait("vector", osem)
        p.op("vector", lambda e, t=t, ti=ti: e.scalar_tensor_tensor(
            out=g.tt(1, F32)[:], in0=yacc[:, ti * 2048:(ti + 1) * 2048],
            scalar=g.xscol[:, 32 + (t % 16):33 + (t % 16)], in1=xt[:],
            op0=AO.mult, op1=AO.add), inc=("cv", 1))
        p.wait("gpsimd", "cv")
        p.op("gpsimd", lambda e, t=t: e.dma_start(
            out=dst[t * 128:(t + 1) * 128, :], in_=g.tt(1, F32)[:]),
            inc=(osem, 16))
    p.fence()


def emit_latent(p, g):
    wkvd = g.pa(F32)[:, 0:4096]
    p.op("sync", lambda e: e.dma_start(
        out=wkvd.rearrange("q (c r) -> q c r", c=16),
        in_=g.wkvd_in.rearrange("(c q) r -> q c r", q=128)),
        inc=("sd", 16))
    lnT = g.tt(3, F32)
    for t in range(NT):
        xt = g.tt(0, F32)
        lnt = g.tt(1, F32)
        sq = g.tt(2, F32)
        st = g.stat
        p.wait("sync", "cv", p.tick("cv"))
        p.wait("sync", "cs", p.tick("cs"))
        p.wait("sync", "ct", p.tick("ct"))
        p.wait("sync", "sg", p.tick("sg"))
        p.op("sync", lambda e, t=t: e.dma_start(
            out=xt[:], in_=g.x1_s[t * 128:(t + 1) * 128, :]), inc=("sd", 16))
        p.wait("vector", "sd")
        p.op("vector", lambda e: e.reduce_sum(st[:, 0:1], xt[:], axis=AX.X),
             inc=("cv", 1))
        p.wait("scalar", "sd")
        p.op("scalar", lambda e: e.activation(out=sq[:], in_=xt[:], func=AF.Square,
             accum_out=st[:, 1:2]), inc=("cs", 1))
        p.wait("vector", "cv")
        p.wait("vector", "cs")
        p.op("vector", lambda e: e.tensor_scalar(
            out=st[:, 2:3], in0=st[:, 0:1], scalar1=float(1.0 / D), scalar2=None,
            op0=AO.mult))
        p.op("vector", lambda e: e.tensor_scalar(
            out=st[:, 3:4], in0=st[:, 1:2], scalar1=float(1.0 / D), scalar2=None,
            op0=AO.mult), inc=("cv", 1))
        p.wait("vector", "cv")
        p.op("vector", lambda e: e.tensor_tensor(
            out=st[:, 4:5], in0=st[:, 2:3], in1=st[:, 2:3], op=AO.mult),
            inc=("cv", 1))
        p.wait("vector", "cv")
        p.op("vector", lambda e: e.tensor_sub(st[:, 5:6], st[:, 3:4], st[:, 4:5]),
             inc=("cv", 1))
        p.wait("scalar", "cv")
        p.op("scalar", lambda e: e.activation(out=st[:, 6:7], in_=st[:, 5:6],
             func=AF.Sqrt, bias=g.eps5[:]), inc=("cs", 1))
        p.wait("vector", "cs")
        p.op("vector", lambda e: e.reciprocal(st[:, 7:8], st[:, 6:7]), inc=("cv", 1))
        p.wait("vector", "cv")
        p.op("vector", lambda e: e.scalar_tensor_tensor(
            out=st[:, 8:9], in0=st[:, 2:3], scalar=-1.0, in1=st[:, 7:8],
            op0=AO.mult, op1=AO.mult), inc=("cv", 1))
        p.wait("scalar", "cv")
        p.op("scalar", lambda e: e.activation(out=lnt[:], in_=xt[:],
             func=AF.Identity, scale=st[:, 7:8], bias=st[:, 8:9]), inc=("cs", 1))
        p.wait("gpsimd", "cs")
        p.op("gpsimd", lambda e, t=t: e.dma_start(
            out=g.ln1_s[t * 128:(t + 1) * 128, :], in_=lnt[:]), inc=("sg", 16))
        p.wait("tensor", "cs")
        for c in range(16):
            ps = g.psB[:, 0:128] if c % 2 == 0 else g.psB[:, 512:640]
            if c >= 2:
                p.wait("tensor", "cv", p.tick("cv") - 1)
            p.op("tensor", lambda e, c=c, ps=ps: e.transpose(
                ps, lnt[:, c * 128:(c + 1) * 128], g.ident[:]), inc=("ct", 1))
            p.wait("vector", "ct")
            p.op("vector", lambda e, c=c, ps=ps: e.tensor_copy(
                lnT[:, c * 128:(c + 1) * 128], ps), inc=("cv", 1))
        p.wait("tensor", "cv")
        p.wait("tensor", "cs")
        p.wait("tensor", "sd")
        for c in range(16):
            p.op("tensor", lambda e, c=c: e.matmul(
                g.psA[:, 0:256], lnT[:, c * 128:(c + 1) * 128],
                wkvd[:, c * 256:(c + 1) * 256], start=(c == 0), stop=(c == 15)),
                inc=("ct", 1) if c == 15 else None)
        p.wait("scalar", "ct")
        p.op("scalar", lambda e: e.activation(out=g.tt(2, F32)[:, 0:256],
             in_=g.psA[:, 0:256], func=AF.Square, accum_out=st[:, 9:10]),
             inc=("cs", 1))
        p.wait("vector", "cs")
        p.op("vector", lambda e: e.tensor_scalar(
            out=st[:, 10:11], in0=st[:, 9:10], scalar1=float(1.0 / R),
            scalar2=g.eps6[:], op0=AO.mult, op1=AO.add), inc=("cv", 1))
        p.wait("scalar", "cv")
        p.op("scalar", lambda e: e.activation(out=st[:, 12:13], in_=st[:, 10:11],
             func=AF.Sqrt), inc=("cs", 1))
        p.wait("vector", "cs")
        p.op("vector", lambda e: e.reciprocal(st[:, 13:14], st[:, 12:13]),
             inc=("cv", 1))
        p.wait("scalar", "cv")
        p.op("scalar", lambda e: e.activation(out=g.tt(2, F32)[:, 0:256],
             in_=g.psA[:, 0:256], func=AF.Identity, scale=st[:, 13:14]),
             inc=("cs", 1))
        p.wait("vector", "cs")
        p.op("vector", lambda e: e.tensor_tensor(
            out=g.tt(2, F32)[:, 256:512], in0=g.tt(2, F32)[:, 0:256],
            in1=g.kvwb[:], op=AO.mult), inc=("cv", 1))
        p.wait("tensor", "cv")
        for c in range(2):
            ps = g.psB[:, 256:384]
            p.op("tensor", lambda e, c=c, ps=ps: e.transpose(
                ps, g.tt(2, F32)[:, 256 + c * 128:256 + (c + 1) * 128], g.ident[:]),
                inc=("ct", 1))
            p.wait("vector", "ct")
            p.op("vector", lambda e, c=c, t=t, ps=ps: e.tensor_copy(
                g.latT[:, c * T + t * 128:c * T + (t + 1) * 128], ps), inc=("cv", 1))
            p.wait("tensor", "cv")


def emit_qproj(p, g):
    lnqT = g.pa(F32)
    for t in range(NQT):
        gt = g.tt(0, F32)
        p.wait("gpsimd", "cv", p.tick("cv"))
        lt = p.op("gpsimd", lambda e, t=t: e.indirect_dma_start(
            out=gt[:], out_offset=None, in_=g.ln1_s[:, :],
            in_offset=IndirectOffsetOnAxis(g.qidx_t[:, t:t + 1], 0)),
            inc=("g1", 16))
        p.wait("tensor", "g1", lt)
        for c in range(16):
            ps = g.psB[:, 0:128] if c % 2 == 0 else g.psB[:, 512:640]
            if c >= 2:
                p.wait("tensor", "cv", p.tick("cv") - 1)
            p.op("tensor", lambda e, c=c, ps=ps: e.transpose(
                ps, gt[:, c * 128:(c + 1) * 128], g.ident[:]), inc=("ct", 1))
            p.wait("vector", "ct")
            p.op("vector", lambda e, c=c, t=t, ps=ps: e.tensor_copy(
                lnqT[:, c * 1024 + t * 128:c * 1024 + (t + 1) * 128], ps),
                inc=("cv", 1))
        p.wait("gpsimd", "cv")
    p.fence()
    wqb = [g.pb(F32)[:, 0:8192], g.pb(F32)[:, 8192:16384]]

    def load_wq(ab, which):
        p.op(which, lambda e, ab=ab: e.dma_start(
            out=wqb[ab % 2].rearrange("q (c f) -> q c f", c=16),
            in_=g.wq_in[:, ab * 512:(ab + 1) * 512]
            .rearrange("(c q) f -> q c f", q=128)),
            inc=("sd" if which == "sync" else "sq", 16))

    load_wq(0, "sync")
    for ab in range(4):
        p.wait("tensor", "sd")
        p.wait("tensor", "sq")
        if ab + 1 < 4:
            q = "scalar" if (ab + 1) % 2 else "sync"
            p.wait(q, "ct", p.tick("ct"))
            load_wq(ab + 1, q)
        for t in range(NQT):
            pm = g.psA[:, 0:512] if t % 2 == 0 else g.psA[:, 512:1024]
            p.wait("tensor", "cv", p.tick("cv") - 1)
            for c in range(16):
                p.op("tensor", lambda e, ab=ab, t=t, c=c, pm=pm: e.matmul(
                    pm, lnqT[:, c * 1024 + t * 128:c * 1024 + (t + 1) * 128],
                    wqb[ab % 2][:, c * 512:(c + 1) * 512],
                    start=(c == 0), stop=(c == 15)),
                    inc=("ct", 1) if c == 15 else None)
            p.wait("vector", "ct")
            p.wait("vector", "sg")
            p.op("vector", lambda e, pm=pm: e.tensor_copy(g.tt(1, F32)[:, 0:512], pm),
                 inc=("cv", 1))
            p.wait("gpsimd", "cv")
            p.op("gpsimd", lambda e, ab=ab, t=t: e.dma_start(
                out=g.q_s[t * 128:(t + 1) * 128, ab * 512:(ab + 1) * 512],
                in_=g.tt(1, F32)[:, 0:512]), inc=("sg", 16))
    p.fence()


def emit_attention(p, g):
    wkvu = g.pa(F32)[:, 0:8192]
    p.op("sync", lambda e: e.dma_start(
        out=wkvu.rearrange("q (c f) -> q c f", c=2),
        in_=g.wkvu_in.rearrange("(c q) f -> q c f", q=128)),
        inc=("sd", 16))
    masks = g.pb(F32)[:, 8192:16384]
    p.op("sync", lambda e: e.dma_start(
        out=masks.rearrange("q (a f) -> q a f", a=8),
        in_=g.mask_in.rearrange("(a q) f -> q a f", q=128)),
        inc=("sd", 16))
    kT = g.tt(0, F32)
    vb = g.tt(1, F32)
    Sb = g.tt(2, F32)
    Pb = g.tt(3, F32)
    PT = g.pb(F32)[:, 0:8192]
    att_st = g.pa(F32)[:, 8704:9216]
    qstg = g.pa(F32)[:, 9216:10240]
    qTh = g.pa(F32)[:, 10240:11264]
    for h in range(H):
        p.wait("tensor", "sd")
        p.wait("tensor", "cv", p.tick("cv"))
        for yb in range(4):
            pm = g.psA[:, yb * 512:(yb + 1) * 512]
            for c in range(2):
                p.op("tensor", lambda e, h=h, yb=yb, c=c, pm=pm: e.matmul(
                    pm, wkvu[:, c * 4096 + h * 128:c * 4096 + (h + 1) * 128],
                    g.latT[:, c * T + yb * 512:c * T + (yb + 1) * 512],
                    start=(c == 0), stop=(c == 1)),
                    inc=("ct", 1) if (yb == 3 and c == 1) else None)
        p.wait("vector", "ct")
        p.op("vector", lambda e: e.tensor_copy(kT[:], g.psA[:, 0:2048]),
             inc=("cv", 1))
        for k in range(16):
            pm = g.psB[:, 0:128] if k % 2 == 0 else g.psB[:, 512:640]
            if k >= 2:
                p.wait("tensor", "cv", p.tick("cv") - 1)
            for c in range(2):
                p.op("tensor", lambda e, h=h, k=k, c=c, pm=pm: e.matmul(
                    pm, g.latT[:, c * T + k * 128:c * T + (k + 1) * 128],
                    wkvu[:, c * 4096 + 2048 + h * 128:c * 4096 + 2048 + (h + 1) * 128],
                    start=(c == 0), stop=(c == 1)),
                    inc=("ct", 1) if c == 1 else None)
            p.wait("vector", "ct")
            p.op("vector", lambda e, k=k, pm=pm: e.tensor_copy(
                vb[:, k * 128:(k + 1) * 128], pm), inc=("cv", 1))
        p.wait("sync", "ct", p.tick("ct"))
        qlt = []
        for u in range(8):
            qlt.append(p.op("sync", lambda e, u=u, h=h: e.dma_start(
                out=qstg[:, u * 128:(u + 1) * 128],
                in_=g.q_s[u * 128:(u + 1) * 128, h * 128:(h + 1) * 128]),
                inc=("sd", 16)))
        for u in range(8):
            ps = g.psB[:, 256:384] if u % 2 == 0 else g.psB[:, 768:896]
            p.wait("tensor", "sd", qlt[7])
            if u >= 2:
                p.wait("tensor", "cv", p.tick("cv") - 1)
            p.op("tensor", lambda e, u=u, ps=ps: e.transpose(
                ps, qstg[:, u * 128:(u + 1) * 128], g.ident[:]), inc=("ct", 1))
            p.wait("vector", "ct")
            p.op("vector", lambda e, u=u, ps=ps: e.tensor_copy(
                qTh[:, u * 128:(u + 1) * 128], ps), inc=("cv", 1))
        for s in range(2):
            smax = EXT[s][3]
            nyt = smax // 128
            p.wait("vector", "ct", p.tick("ct"))
            ragged = [(j, k) for j in range(4) for k in range(EXT[s][j] // 128, nyt)]
            for n, (j, k) in enumerate(ragged):
                p.op("vector", lambda e, k=k, j=j: e.memset(
                    PT[:, k * 512 + j * 128:k * 512 + (j + 1) * 128], 0.0),
                    inc=("cv", 1) if n == len(ragged) - 1 else None)
            for j in range(4):
                E = EXT[s][j]
                xt_global = s * 4 + j
                qTu = qTh[:, xt_global * 128:(xt_global + 1) * 128]
                p.wait("tensor", "cv", p.tick("cv"))
                nblk = ceil_div(E, 512)
                for kb in range(nblk):
                    w = min(512, E - kb * 512)
                    pm = g.psB[:, 0:512] if kb % 2 == 0 else g.psB[:, 512:1024]
                    if kb >= 2:
                        p.wait("tensor", "cv", p.tick("cv") - 1)
                    p.op("tensor", lambda e, kb=kb, w=w, pm=pm, qTu=qTu: e.matmul(
                        pm[:, 0:w], qTu, kT[:, kb * 512:kb * 512 + w],
                        start=True, stop=True), inc=("ct", 1))
                    p.wait("vector", "ct")
                    mstart = E - 1024
                    b0 = kb * 512
                    if b0 >= mstart:
                        mj = xt_global * 1024 + (b0 - mstart)
                        p.op("vector", lambda e, kb=kb, w=w, b0=b0, mj=mj, pm=pm:
                             e.scalar_tensor_tensor(
                                 out=Sb[:, b0:b0 + w], in0=pm[:, 0:w], scalar=1.0,
                                 in1=masks[:, mj:mj + w], op0=AO.mult, op1=AO.add),
                             inc=("cv", 1))
                    else:
                        p.op("vector", lambda e, kb=kb, w=w, b0=b0, pm=pm:
                             e.tensor_copy(Sb[:, b0:b0 + w], pm[:, 0:w]),
                             inc=("cv", 1))
                p.wait("vector", "cv")
                p.op("vector", lambda e, E=E: e.reduce_max(
                    g.stat[:, 0:1], Sb[:, 0:E], axis=AX.X), inc=("cv", 1))
                p.wait("scalar", "cv")
                p.wait("scalar", "ct")
                p.op("scalar", lambda e: e.activation(
                    out=g.stat[:, 1:2], in_=g.stat[:, 0:1], func=AF.Identity,
                    scale=-SC), inc=("cs", 1))
                p.wait("scalar", "cs")
                p.op("scalar", lambda e, E=E: e.activation(
                    out=Pb[:, 0:E], in_=Sb[:, 0:E], func=AF.Exp,
                    scale=SC, bias=g.stat[:, 1:2], accum_out=g.stat[:, 2:3]),
                    inc=("cs", 1))
                p.wait("vector", "cs")
                p.op("vector", lambda e: e.reciprocal(g.stat[:, 3:4], g.stat[:, 2:3]),
                     inc=("cv", 1))
                p.wait("vector", "cv")
                p.op("vector", lambda e, E=E: e.tensor_scalar(
                    out=Sb[:, 0:E], in0=Pb[:, 0:E], scalar1=g.stat[:, 3:4],
                    scalar2=None, op0=AO.mult), inc=("cv", 1))
                p.wait("tensor", "cv")
                for k in range(E // 128):
                    ps = g.psB[:, 1024:1152] if k % 2 == 0 else \
                        g.psB[:, 1536:1664]
                    if k >= 2:
                        p.wait("tensor", "cv", p.tick("cv") - 1)
                    p.op("tensor", lambda e, k=k, ps=ps: e.transpose(
                        ps, Sb[:, k * 128:(k + 1) * 128], g.ident[:]), inc=("ct", 1))
                    p.wait("vector", "ct")
                    p.op("vector", lambda e, k=k, j=j, ps=ps: e.tensor_copy(
                        PT[:, k * 512 + j * 128:k * 512 + (j + 1) * 128], ps),
                        inc=("cv", 1))
                p.wait("tensor", "cv")
            pm = g.psA[:, 0:512] if s == 0 else g.psA[:, 512:1024]
            for k in range(nyt):
                p.op("tensor", lambda e, k=k, pm=pm: e.matmul(
                    pm, vb[:, k * 128:(k + 1) * 128], PT[:, k * 512:(k + 1) * 512],
                    start=(k == 0), stop=(k == nyt - 1)),
                    inc=("ct", 1) if k == nyt - 1 else None)
            p.wait("vector", "ct")
            p.wait("vector", "sg")
            p.op("vector", lambda e, pm=pm: e.tensor_copy(att_st, pm), inc=("cv", 1))
            p.wait("gpsimd", "cv")
            p.op("gpsimd", lambda e, h=h, s=s: e.dma_start(
                out=g.attn_s[h * 128:(h + 1) * 128, s * 512:(s + 1) * 512],
                in_=att_st), inc=("sg", 16))
        p.wait("tensor", "sg", p.tick("sg"))


def emit_wo(p, g):
    x1q = g.pa(F32)
    for t in range(NQT):
        p.wait("gpsimd", "cv", p.tick("cv"))
        p.wait("gpsimd", "ct", p.tick("ct"))
        p.op("gpsimd", lambda e, t=t: e.indirect_dma_start(
            out=x1q[:, t * 2048:(t + 1) * 2048], out_offset=None,
            in_=g.x1_s[:, :],
            in_offset=IndirectOffsetOnAxis(g.qidx_t[:, t:t + 1], 0)),
            inc=("g1", 16))
    p.fence()
    wob = [g.pb(F32)[:, 0:8192], g.pb(F32)[:, 8192:16384]]

    def load_wo(ob, which):
        p.op(which, lambda e, ob=ob: e.dma_start(
            out=wob[ob % 2].rearrange("q (c f) -> q c f", c=16),
            in_=g.wo_in[:, ob * 512:(ob + 1) * 512]
            .rearrange("(c q) f -> q c f", q=128)),
            inc=("sd" if which == "sync" else "sq", 16))

    load_wo(0, "sync")
    aT = g.tt(0, F32)
    for ob in range(4):
        p.wait("tensor", "sd")
        p.wait("tensor", "sq")
        if ob + 1 < 4:
            q = "scalar" if (ob + 1) % 2 else "sync"
            p.wait(q, "ct", p.tick("ct"))
            load_wo(ob + 1, q)
        for t in range(NQT):
            p.wait("sync", "ct", p.tick("ct"))
            lt = p.op("sync", lambda e, t=t: e.dma_start(
                out=aT[:].rearrange("q (c f) -> q c f", c=16),
                in_=g.attn_s[:, t * 128:(t + 1) * 128]
                .rearrange("(c q) f -> q c f", q=128)), inc=("sa", 16))
            p.wait("tensor", "sa", lt)
            pm = g.psA[:, 0:512] if t % 2 == 0 else g.psA[:, 512:1024]
            p.wait("tensor", "cv", p.tick("cv") - 1)
            for c in range(16):
                p.op("tensor", lambda e, t=t, ob=ob, c=c, pm=pm: e.matmul(
                    pm, aT[:, c * 128:(c + 1) * 128],
                    wob[ob % 2][:, c * 512:(c + 1) * 512],
                    start=(c == 0), stop=(c == 15)),
                    inc=("ct", 1) if c == 15 else None)
            p.wait("vector", "ct")
            p.wait("vector", "sg")
            p.op("vector", lambda e, t=t, ob=ob, pm=pm: e.scalar_tensor_tensor(
                out=g.tt(1, F32)[:, 0:512], in0=pm, scalar=1.0,
                in1=x1q[:, t * 2048 + ob * 512:t * 2048 + (ob + 1) * 512],
                op0=AO.mult, op1=AO.add), inc=("cv", 1))
            p.wait("gpsimd", "cv")
            p.op("gpsimd", lambda e, t=t, ob=ob: e.dma_start(
                out=g.x2_s[t * 128:(t + 1) * 128, ob * 512:(ob + 1) * 512],
                in_=g.tt(1, F32)[:, 0:512]), inc=("sg", 16))
    p.fence()



def build_program(dev=None):
    nc = bass.Bass("TRN2", target_bir_lowering=False, debug=False)

    class NS:
        pass

    g = NS()
    g.dev = dev
    g.x_in = nc.dram_tensor("x", [T, D], F32, kind="ExternalInput").ap()
    g.wg_in = nc.dram_tensor("wg", [FI, D], F32, kind="ExternalInput").ap()
    g.wu_in = nc.dram_tensor("wu", [FI, D], F32, kind="ExternalInput").ap()
    g.wd_in = nc.dram_tensor("wd", [D, FI], F32, kind="ExternalInput").ap()
    g.wq_in = nc.dram_tensor("wqa", [D, D], F32, kind="ExternalInput").ap()
    g.wkvd_in = nc.dram_tensor("wkvd", [D, R], F32, kind="ExternalInput").ap()
    g.kvw_in = nc.dram_tensor("kvw", [1, R], F32, kind="ExternalInput").ap()
    g.wkvu_in = nc.dram_tensor("wkvu", [R, 2 * D], F32, kind="ExternalInput").ap()
    g.wo_in = nc.dram_tensor("wo", [D, D], F32, kind="ExternalInput").ap()
    g.qidx_in = nc.dram_tensor("qidx", [TQ, 1], I32, kind="ExternalInput").ap()
    g.mask_in = nc.dram_tensor("masks", [8 * 128, 1024], F32,
                               kind="ExternalInput").ap()
    g.out = nc.dram_tensor("x3", [TQ, D], F32, kind="ExternalOutput").ap()
    g.dev_out = None
    if dev is not None:
        shp = {"consts": [128, R], "means": [128, 8], "quant": [256, D],
               "quantd": [256, FI],
               "ffnA": [128, 2048], "h": [256, FI], "wblk": [128, 2048],
               "x1": [T, D], "ln1": [T, D], "latT": [128, 2 * T], "q": [TQ, D],
               "attn": [D, TQ], "x2": [TQ, D]}[dev]
        g.dev_out = nc.dram_tensor("dbg", shp, F32, kind="ExternalOutput").ap()

    g.wgq_s = nc.dram_tensor("wgq_s", [FI, D], BF16).ap()
    g.wuq_s = nc.dram_tensor("wuq_s", [FI, D], BF16).ap()
    g.wdq_s = nc.dram_tensor("wdq_s", [D, FI], BF16).ap()
    g.wdqT_s = nc.dram_tensor("wdqT_s", [FI, D], BF16).ap()
    g.h_s = nc.dram_tensor("h_s", [T, FI], F32).ap()
    g.x1_s = nc.dram_tensor("x1_s", [T, D], F32).ap()
    g.ln1_s = nc.dram_tensor("ln1_s", [T, D], F32).ap()
    g.q_s = nc.dram_tensor("q_s", [TQ, D], F32).ap()
    g.attn_s = nc.dram_tensor("attn_s", [D, TQ], F32).ap()
    g.x2_s = nc.dram_tensor("x2_s", [TQ, D], F32).ap()

    from contextlib import ExitStack
    with ExitStack() as stack:
        sems = {}
        for name in ("sd", "sq", "sg", "cv", "cs", "ct", "cg", "st",
                     "sa", "sb", "sc", "se", "ha", "hb", "g1"):
            sems[name] = stack.enter_context(nc.semaphore(name))

        sb = lambda n, shp, dt: stack.enter_context(nc.sbuf_tensor(n, list(shp), dt))
        g.ident = sb("ident", (128, 128), F32)
        g.identb = sb("identb", (128, 128), BF16)
        g.kvwb = sb("kvwb", (128, R), F32)
        g.c127r = sb("c127r", (128, 1), F32)
        g.eps5 = sb("eps5", (128, 1), F32)
        g.eps6 = sb("eps6", (128, 1), F32)
        g.ones = sb("ones", (128, 1), F32)
        g.onesr = sb("onesr", (1, 128), F32)
        g.stat = sb("stat", (128, 16), F32)
        g.msum = sb("msum", (128, 54), F32)
        g.zero1 = sb("zero1", (128, 1), F32)
        g.ws3 = sb("ws3", (128, 8), F32)
        g.xscol = sb("xscol", (128, 48), F32)
        g.hmx = sb("hmx", (128, 256), F32)
        g.qidx_t = sb("qidx_t", (128, 8), I32)
        g.recipd = sb("recipd", (128, 8), F32)

        PA = sb("PA", (128, 16384), F32)
        PB = sb("PB", (128, 16384), F32)
        PC = sb("PC", (128, 8192), F32)
        TT = sb("TT", (128, 8192), F32)
        g._PA, g._PB, g._PC, g._TT = PA, PB, PC, TT
        g.pa = lambda dt: PA[:].bitcast(dt) if dt != F32 else PA[:]
        g.pb = lambda dt: PB[:].bitcast(dt) if dt != F32 else PB[:]
        g.pc = lambda dt: PC[:].bitcast(dt) if dt != F32 else PC[:]
        g.latT = PC[:, 0:2 * T]
        g.tt = lambda i, dt: (TT[:] if i is None else
                              TT[:, i * 2048:(i + 1) * 2048])
        g.tt_q = lambda dt: TT[:, 6144:6272]

        g.psA = stack.enter_context(nc.psum_tensor("psA", [128, 2048], F32))[:]
        g.psB = stack.enter_context(nc.psum_tensor("psB", [128, 2048], F32))[:]

        block = stack.enter_context(nc.Block())

        for name in ("sync", "scalar", "vector", "tensor", "gpsimd"):
            def make(nm):
                def f(e):
                    emit_all(P(nm, e, sems), g)
                return f
            getattr(block, name)(make(name))

    return nc



_CHUNKS = [[0, 3], [1, 2]]


def _host_meta():
    metas = []
    for role in range(2):
        ch = _CHUNKS[role]
        qidx = np.concatenate([np.arange(c * 512, (c + 1) * 512, dtype=np.int32)
                               for c in ch])[:, None]
        masks = np.zeros((8 * 128, 1024), dtype=np.float32)
        for s in range(2):
            for j in range(4):
                E = EXT[s][j]
                qpos = qidx[s * 512 + j * 128: s * 512 + (j + 1) * 128, 0]
                ycols = np.arange(E - 1024, E)
                m = np.where(ycols[None, :] <= qpos[:, None], 0.0, NEG)
                masks[(s * 4 + j) * 128:(s * 4 + j + 1) * 128, :] = m
        metas.append((qidx, masks))
    return metas


_NC_CACHE = {}


def _get_nc(dev=None):
    if dev not in _NC_CACHE:
        _NC_CACHE[dev] = build_program(dev)
    return _NC_CACHE[dev]


def kernel(L0_attn_x, L0_attn_Wq, L0_attn_Wkv_down, L0_attn_kv_norm_w,
           L0_attn_Wkv_up, L0_attn_Wo, L0_ffn_w_gate, L0_ffn_w_up,
           L0_ffn_w_down, _dev=None):
    x = np.ascontiguousarray(np.asarray(L0_attn_x, dtype=np.float32))
    wq = np.ascontiguousarray(np.asarray(L0_attn_Wq, dtype=np.float32))
    wkvd = np.ascontiguousarray(np.asarray(L0_attn_Wkv_down, dtype=np.float32))
    kvw = np.ascontiguousarray(np.asarray(L0_attn_kv_norm_w,
                                          dtype=np.float32)).reshape(1, R)
    wkvu = np.ascontiguousarray(np.asarray(L0_attn_Wkv_up, dtype=np.float32))
    wo = np.ascontiguousarray(np.asarray(L0_attn_Wo, dtype=np.float32))
    wg = np.ascontiguousarray(np.asarray(L0_ffn_w_gate, dtype=np.float32))
    wu = np.ascontiguousarray(np.asarray(L0_ffn_w_up, dtype=np.float32))
    wd = np.ascontiguousarray(np.asarray(L0_ffn_w_down, dtype=np.float32))

    metas = _host_meta()
    nc = _get_nc(_dev)
    in_maps = []
    for core in range(8):
        b, role = core // 2, core % 2
        qidx, masks = metas[role]
        in_maps.append({
            "x": np.ascontiguousarray(x[b]),
            "wg": wg, "wu": wu, "wd": wd, "wqa": wq, "wkvd": wkvd,
            "kvw": kvw, "wkvu": wkvu, "wo": wo,
            "qidx": qidx, "masks": masks,
        })
    res = run_bass_kernel_spmd(nc, in_maps, core_ids=list(range(8)))
    if _dev is not None:
        return res
    out = np.empty((4, T, D), dtype=np.float32)
    for core in range(8):
        b, role = core // 2, core % 2
        x3 = res.results[core]["x3"]
        for si, c in enumerate(_CHUNKS[role]):
            out[b, c * 512:(c + 1) * 512, :] = x3[si * 512:(si + 1) * 512, :]
    return out
